# version 21
# baseline (speedup 1.0000x reference)
import math
from contextlib import ExitStack

import ml_dtypes
import numpy as np

import concourse.bass as bass
import concourse.mybir as mybir
from concourse.ap import AP
from concourse.bass_utils import run_bass_kernel_spmd

F32 = mybir.dt.float32
F32R = mybir.dt.float32r
BF16 = mybir.dt.bfloat16
I32 = mybir.dt.int32
AF = mybir.ActivationFunctionType
ALU = mybir.AluOpType
AXX = mybir.AxisListType.X

SEQ = 2048
D = 1024
NH = 16
NT = SEQ // 128
NCH = D // 128
NP = NH // 2
NQC = SEQ // 512
EXPB = np.float32(-12.0)
F1 = np.float32(1.0)
F0 = np.float32(0.0)


class Sched:
    ENGINES = ("tensor", "vector", "scalar", "gpsimd", "sync")
    NDMA = 8
    BARRIER_DMA = ("sync", "scalar")

    def __init__(self):
        self.steps = []

    enabled = True

    def add(self, eng, fn, reads=(), writes=(), dma=False):
        if not self.enabled:
            return
        self.steps.append([eng, fn, tuple(reads), tuple(writes), bool(dma)])

    def barrier(self):
        self.steps.append(["BARRIER", None, (), (), False])

    def build(self, block, sems, serial=False):
        steps = self.steps
        n = len(steps)
        cpos, dpos = {}, {}
        ccount = {e: 0 for e in self.ENGINES}
        dcount = {e: 0 for e in self.ENGINES}
        deps = [set() for _ in range(n)]
        last_w, readers = {}, {}
        dqueues = {}
        bar_after = [None] * n
        snap = None
        for i, (eng, fn, rd, wr, dma) in enumerate(steps):
            if eng == "BARRIER":
                snap = (dict(ccount), dict(dcount))
                continue
            bar_after[i] = snap
            if dma:
                dpos[i] = dcount[eng]
                dcount[eng] += 1
            else:
                cpos[i] = ccount[eng]
                ccount[eng] += 1
            d = deps[i]
            for r in rd:
                j = last_w.get(r)
                if j is not None:
                    d.add(j)
            for w in wr:
                j = last_w.get(w)
                if j is not None:
                    d.add(j)
                for j in readers.get(w, ()):
                    d.add(j)
            if dma:
                q = dqueues.setdefault(eng, [])
                if len(q) >= self.NDMA:
                    d.add(q[-self.NDMA])
                q.append(i)
            for w in wr:
                last_w[w] = i
                readers[w] = []
            for r in rd:
                readers.setdefault(r, []).append(i)

        sched = self

        def mk(engine_name):
            def emit(eng):
                watermark = {}

                def wait_val(key, sem, val):
                    if watermark.get(key, -1) >= val:
                        return
                    watermark[key] = val
                    eng.wait_ge(sem, val)

                def wait_step(j):
                    e2, _, _, _, dm2 = steps[j]
                    if dm2:
                        k = dpos[j]
                        key = f"d:{e2}:{k % sched.NDMA}"
                        wait_val(key, sems[key], 16 * (k // sched.NDMA + 1))
                    elif e2 != engine_name or e2 != "tensor":
                        key = f"c:{e2}"
                        wait_val(key, sems[key], cpos[j] + 1)

                emitted_bar = None
                run_cc = {e2: 0 for e2 in sched.ENGINES}
                run_dc = {e2: 0 for e2 in sched.ENGINES}
                for i, (e, fn, rd, wr, dma) in enumerate(steps):
                    if e == "BARRIER":
                        continue
                    if serial and e != engine_name:
                        if dma:
                            run_dc[e] += 1
                        else:
                            run_cc[e] += 1
                        continue
                    if serial:
                        for e2 in sched.ENGINES:
                            if run_cc[e2] > 0:
                                wait_val(f"c:{e2}", sems[f"c:{e2}"], run_cc[e2])
                            for kk in range(max(0, run_dc[e2] - sched.NDMA), run_dc[e2]):
                                key = f"d:{e2}:{kk % sched.NDMA}"
                                wait_val(key, sems[key],
                                         16 * (kk // sched.NDMA + 1))
                        if dma:
                            run_dc[e] += 1
                        else:
                            run_cc[e] += 1
                        ins = fn(eng)
                        if dma:
                            k = dpos[i]
                            ins.then_inc(sems[f"d:{engine_name}:{k % sched.NDMA}"], 16)
                        else:
                            ins.then_inc(sems[f"c:{engine_name}"], 1)
                        continue
                    if e != engine_name:
                        continue
                    if bar_after[i] is not None and bar_after[i] is not emitted_bar:
                        cc, dc = bar_after[i]
                        emitted_bar = bar_after[i]
                        for e2 in sched.ENGINES:
                            if e2 != engine_name and cc[e2] > 0:
                                wait_val(f"c:{e2}", sems[f"c:{e2}"], cc[e2])
                        for e2 in sched.BARRIER_DMA:
                            for kk in range(max(0, dc[e2] - sched.NDMA), dc[e2]):
                                key = f"d:{e2}:{kk % sched.NDMA}"
                                wait_val(key, sems[key],
                                         16 * (kk // sched.NDMA + 1))
                    for j in sorted(deps[i]):
                        wait_step(j)
                    ins = fn(eng)
                    if dma:
                        k = dpos[i]
                        ins.then_inc(sems[f"d:{engine_name}:{k % sched.NDMA}"], 16)
                    else:
                        ins.then_inc(sems[f"c:{engine_name}"], 1)
            return emit

        block.tensor(mk("tensor"))
        block.vector(mk("vector"))
        block.scalar(mk("scalar"))
        block.gpsimd(mk("gpsimd"))
        block.sync(mk("sync"))


def _oh_const():
    d = np.arange(128)
    t = np.log(np.maximum(d, 1).astype(np.float32) / np.float32(16.0))
    t = (t / np.float32(math.log(8.0))).astype(np.float32)
    t = (t * np.float32(16.0)).astype(np.float32)
    vl = np.minimum(16 + t.astype(np.int32), 31)
    bucket = np.where(d < 16, d, vl)
    oh = np.zeros((32, 128), np.float32)
    scale = np.float32((D // NH) ** (-0.5))
    for dlt in range(128):
        oh[bucket[dlt], dlt] = scale
    return oh


def build_nc(serial=False, upto=3):
    nc = bass.Bass("TRN2", target_bir_lowering=False)
    dt = nc.dram_tensor
    x_in = dt("x", [SEQ, D], F32, kind="ExternalInput")
    wq = dt("wq", [D, D], F32, kind="ExternalInput")
    wk = dt("wk", [D, D], F32, kind="ExternalInput")
    wv = dt("wv", [D, D], F32, kind="ExternalInput")
    wo = dt("wo", [D, D], F32, kind="ExternalInput")
    wm = dt("wm", [D, 6 * D], F32, kind="ExternalInput")
    pkt = dt("pk", [D, 256], I32, kind="ExternalInput")
    ccol = dt("ccol", [128, 8], F32, kind="ExternalInput")
    bqc_d = dt("bqc", [128, 8], F32, kind="ExternalInput")
    bkc_d = dt("bkc", [128, 8], F32, kind="ExternalInput")
    bvr = dt("bvr", [1, D], F32, kind="ExternalInput")
    bor = dt("bor", [1, D], F32, kind="ExternalInput")
    fbr = dt("fbr", [1, D], F32, kind="ExternalInput")
    rel = dt("rel", [32, 16], F32, kind="ExternalInput")
    ohd = dt("oh", [32, 128], F32, kind="ExternalInput")
    wsc = dt("wsc", [1, 1], F32, kind="ExternalInput")
    idf = dt("idf", [16, 16], F32, kind="ExternalInput")
    eyed = dt("eye128", [128, 128], BF16, kind="ExternalInput")
    out = dt("out", [SEQ, D], F32, kind="ExternalOutput")
    drows = dt("drows", [1, 8 * D], F32, kind="Internal")
    dsumB2 = dt("dsumB2", [2, 2048], BF16, kind="Internal")
    dsumB3 = dt("dsumB3", [2, 2048], BF16, kind="Internal")
    drow2 = dt("drow2", [1, 4 * D], BF16, kind="Internal")
    dtaus = dt("dtaus", [16, 384], BF16, kind="Internal")
    dtau = dt("dtau", [128, NH * 384], BF16, kind="Internal")

    sq = Sched()
    with ExitStack() as ctx:
        def sb(name, shape, dtp, **kw):
            return ctx.enter_context(nc.sbuf_tensor(name, shape, dtp, **kw))

        sx = sb("sx", [128, NT, D], F32)
        shT = sb("shT", [128, NCH, SEQ], BF16)
        gat = sb("gat", [128, D], BF16)
        swq = sb("swq", [128, 2, NCH, 128], BF16)
        swk = sb("swk", [128, 2, NCH, 128], BF16)
        swv = sb("swv", [128, 2, NCH, 128], BF16)
        swo = sb("swo", [128, NCH, D], BF16)
        sid128 = sb("sid128", [128, 128], BF16)
        sccol = sb("sccol", [128, 8], F32)
        ssil = sb("ssil", [128, 8], F32)
        scsil = sb("scsil", [128, 8], F32R)
        sbqc = sb("sbqc", [128, 8], F32)
        sbkc = sb("sbkc", [128, 8], F32)
        a1c = sb("a1c", [128, 16], F32)
        s_swb = sb("s_swb", [128, 1], F32)
        sbvb = sb("sbvb", [1, D], BF16)
        sbob = sb("sbob", [1, D], BF16)
        swscb = sb("swscb", [1, 1], BF16)
        ones1 = sb("ones1", [1, 128], BF16)
        stats6b = sb("stats6b", [128, NT, 2, 6], F32)
        stats2b = sb("stats2b", [128, NT, 2], F32)
        svar16 = sb("svar16", [128, NT], F32)
        srcol = sb("srcol", [128, NT], F32)
        snmr = sb("snmr", [128, NT], F32)
        skap = sb("skap", [128, NT], F32)
        ssc = sb("ssc", [128, NT], F32)
        stmp1 = sb("stmp1", [128, 2], F32)
        stmp1b = sb("stmp1b", [128, 2], F32)

        sems = {}
        for e in Sched.ENGINES:
            sems[f"c:{e}"] = ctx.enter_context(nc.semaphore(f"c_{e}"))
        for e in ("sync", "gpsimd", "scalar"):
            for k in range(Sched.NDMA):
                sems[f"d:{e}:{k}"] = ctx.enter_context(nc.semaphore(f"d_{e}_{k}"))

        XT = [f"x:{t}" for t in range(NT)]

        def ln_stats(t):
            sq.add("vector", lambda e, t=t: e.bn_stats(
                stats6b[:, t, 0, :], sx[:, t, 0:512]),
                reads=[f"x:{t}"], writes=[f"st6:{t}"])
            sq.add("vector", lambda e, t=t: e.bn_stats(
                stats6b[:, t, 1, :], sx[:, t, 512:1024]),
                reads=[f"x:{t}"], writes=[f"st6:{t}"])
            sq.add("vector", lambda e, t=t: e.bn_aggr(
                stats2b[:, t, :], stats6b[:, t, :, :]),
                reads=[f"st6:{t}"], writes=[f"st2:{t}"])

        def ln_batch():
            st2 = [f"st2:{t}" for t in range(NT)]
            sq.add("vector", lambda e: e.tensor_scalar_add(
                svar16[:], stats2b[:, :, 1], 1e-5),
                reads=st2, writes=["svar"])
            sq.add("scalar", lambda e: e.activation(
                svar16[:], svar16[:], AF.Sqrt, bias=F0, scale=F1),
                reads=["svar"], writes=["svar"])
            sq.add("vector", lambda e: e.reciprocal(srcol[:], svar16[:]),
                   reads=["svar"], writes=["r"])
            sq.add("vector", lambda e: e.tensor_mul(
                snmr[:], stats2b[:, :, 0], srcol[:]),
                reads=st2 + ["r"], writes=["nmr"])
            sq.add("vector", lambda e: e.tensor_scalar_mul(snmr[:], snmr[:], -1.0),
                   reads=["nmr"], writes=["nmr"])

        def ln_apply(t, sytile, sytag):
            sq.add("scalar", lambda e, t=t, sytile=sytile: e.activation(
                sytile, sx[:, t, :], AF.Identity,
                scale=srcol[:, t:t + 1], bias=snmr[:, t:t + 1]),
                reads=[f"x:{t}", "r", "nmr"], writes=[sytag])

        with ExitStack() as a1:
            def sb1(name, shape, dtp):
                return a1.enter_context(nc.sbuf_tensor(name, shape, dtp, side="right"))

            def ps1(name, shape):
                return a1.enter_context(nc.psum_tensor(name, shape, F32))

            swm = sb1("swm", [128, 2, 2048], F32R)
            swm0 = sb1("swm0", [128, 2, 2048], F32)
            scsil0 = sb1("scsil0", [128, 8], F32)
            sy = sb1("sy", [128, 2, D], BF16)
            srow = sb1("srow", [1, D], F32)
            srow2 = sb1("srow2", [1, D], F32)
            srowb = sb1("srowb", [1, D], BF16)
            srel = sb1("srel", [32, 16], F32R)
            soh = sb1("soh", [32, 128], F32R)
            st5 = sb1("st5", [16, 128], F32)
            stau = sb1("stau", [16, 384], F32)
            staub = sb1("staub", [16, 384], BF16)
            srep4 = sb1("srep4", [128, 4, 384], BF16)
            s16 = sb1("s16", [16, 128], F32)
            sidf = sb1("sidf", [16, 16], F32)
            srowst = sb1("srowst", [1, 2, 512], F32)
            p_mod = ps1("p_mod", [1, 2048])
            p_gen = ps1("p_gen", [128, 512])
            p_c16 = ps1("p_c16", [128, 16])
            p_tr = [a1.enter_context(nc.psum_tensor(f"p_tr{i}", [128, 128], BF16))
                    for i in range(2)]

            for t in range(NT):
                sq.add("sync", lambda e, t=t: e.dma_start(
                    out=sx[:, t, :], in_=x_in[128 * t:128 * (t + 1), :]),
                    writes=[f"x:{t}"], dma=True)
            sq.add("sync", lambda e: e.dma_start(out=sccol[:], in_=ccol[:]),
                   writes=["ccol"], dma=True)
            sq.add("sync", lambda e: e.dma_start(out=sbqc[:], in_=bqc_d[:]),
                   writes=["bqc"], dma=True)
            sq.add("sync", lambda e: e.dma_start(out=sbkc[:], in_=bkc_d[:]),
                   writes=["bkc"], dma=True)
            sq.add("sync", lambda e: e.dma_start(out=srow2[:], in_=bvr[:]),
                   writes=["srow2"], dma=True)
            sq.add("vector", lambda e: e.tensor_copy(sbvb[:], srow2[:]),
                   reads=["srow2"], writes=["bvb"])
            sq.add("sync", lambda e: e.dma_start(out=srow[:], in_=bor[:]),
                   writes=["srow"], dma=True)
            sq.add("vector", lambda e: e.tensor_copy(sbob[:], srow[:]),
                   reads=["srow"], writes=["bob"])
            sq.add("gpsimd", lambda e: e.dma_start(out=srel[:], in_=rel[:]),
                   writes=["rel"], dma=True)
            sq.add("gpsimd", lambda e: e.dma_start(out=soh[:], in_=ohd[:]),
                   writes=["oh"], dma=True)
            sq.add("sync", lambda e: e.dma_start(out=sidf[:], in_=idf[:]),
                   writes=["idf"], dma=True)
            sq.add("sync", lambda e: e.dma_start(out=sid128[:], in_=eyed[:]),
                   writes=["sid"], dma=True)
            sq.add("vector", lambda e: e.memset(ones1[:], 1.0), writes=["ones1"])
            sq.add("sync", lambda e: e.dma_start(out=stmp1[0:1, 0:1], in_=wsc[:]),
                   writes=["wscrow"], dma=True)
            sq.add("vector", lambda e: e.tensor_copy(swscb[:], stmp1[0:1, 0:1]),
                   reads=["wscrow"], writes=["wscb"])
            sq.add("tensor", lambda e: e.matmul(
                p_gen[:, 0:1], ones1[:], swscb[:], start=True, stop=True),
                reads=["wscb", "ones1"], writes=["p_gen"])
            sq.add("vector", lambda e: e.tensor_copy(s_swb[:], p_gen[:, 0:1]),
                   reads=["p_gen"], writes=["swb"])

            sq.add("scalar", lambda e: e.activation(ssil[:], sccol[:], AF.Silu, bias=F0, scale=F1),
                   reads=["ccol"], writes=["sil"])
            sq.add("vector", lambda e: e.tensor_copy(scsil[:], ssil[:]),
                   reads=["sil"], writes=["csil"])
            sq.add("vector", lambda e: e.tensor_copy(scsil0[:], ssil[:]),
                   reads=["sil"], writes=["csil0"])

            sq.add("tensor", lambda e: e.matmul(p_gen[0:16, 0:128], srel[:], soh[:],
                                                start=True, stop=True),
                   reads=["rel", "oh"], writes=["p_gen"])
            sq.add("vector", lambda e: e.tensor_copy(st5[:], p_gen[0:16, 0:128]),
                   reads=["p_gen"], writes=["t5"])
            sq.add("vector", lambda e: e.tensor_scalar(
                st5[:], st5[:], st5[:, 127:128], None, ALU.subtract),
                reads=["t5"], writes=["t5"])
            sq.add("vector", lambda e: e.memset(stau[:, 0:127], -1e30), writes=["tau"])
            sq.add("vector", lambda e: e.memset(stau[:, 240:384], 0.0), writes=["tau"])
            sq.add("vector", lambda e: e.tensor_copy(stau[:, 127:240], st5[:, 0:113]),
                   reads=["t5"], writes=["tau"])
            sq.add("scalar", lambda e: e.activation(stau[:], stau[:], AF.Exp, bias=F0, scale=F1),
                   reads=["tau"], writes=["tau"])
            sq.add("vector", lambda e: e.tensor_copy(staub[:], stau[:]),
                   reads=["tau"], writes=["taub"])
            sq.add("sync", lambda e: e.dma_start(out=dtaus[:], in_=staub[:]),
                   reads=["taub"], writes=["dtaus"], dma=True)

            DR = []

            def mod_group(jg):
                qe = ("scalar", "gpsimd", "gpsimd")[jg]
                tile = swm0 if jg == 0 else swm
                col = scsil0 if jg == 0 else scsil
                ctag = "csil0" if jg == 0 else "csil"
                wtag = "swm0" if jg == 0 else "swmr"
                for c in range(NCH):
                    p = (NCH * jg + c) % 2
                    sq.add(qe, lambda e, jg=jg, c=c, p=p, tile=tile: e.dma_start(
                        out=tile[:, p, :],
                        in_=wm[128 * c:128 * (c + 1), 2048 * jg:2048 * (jg + 1)]),
                        writes=[f"{wtag}:{p}"], dma=True)
                    for sl in range(4):
                        sq.add("tensor", lambda e, c=c, sl=sl, p=p, tile=tile,
                               col=col: e.matmul(
                            p_mod[:, 512 * sl:512 * (sl + 1)],
                            col[:, c:c + 1],
                            tile[:, p, 512 * sl:512 * (sl + 1)],
                            start=(c == 0), stop=(c == NCH - 1)),
                            reads=[ctag, f"{wtag}:{p}"],
                            writes=[f"p_mod:{sl}"])
                for sl in range(4):
                    sq.add("scalar", lambda e, sl=sl: e.activation(
                        srowst[0:1, sl % 2, :], p_mod[0:1, 512 * sl:512 * (sl + 1)],
                        AF.Identity, bias=F0, scale=F1),
                        reads=[f"p_mod:{sl}"], writes=[f"srowst:{sl % 2}"])
                    sq.add("scalar", lambda e, jg=jg, sl=sl: e.dma_start(
                        out=drows[:, 2048 * jg + 512 * sl:2048 * jg + 512 * (sl + 1)],
                        in_=srowst[0:1, sl % 2, :]),
                        reads=[f"srowst:{sl % 2}"], writes=[f"dr:{jg}:{sl}"], dma=True)
                    DR.append(f"dr:{jg}:{sl}")

            mod_group(0)
            sq.add("scalar", lambda e: e.dma_start(
                out=s16[:],
                in_=drows[:, 0:2048].rearrange("o (r j) -> (o r) j", j=128)),
                reads=DR[0:4], writes=["s16"], dma=True)
            sq.add("tensor", lambda e: e.transpose(p_c16[:], s16[:], sidf[:]),
                   reads=["s16", "idf"], writes=["p_c16"])
            sq.add("vector", lambda e: e.tensor_copy(a1c[:], p_c16[:]),
                   reads=["p_c16"], writes=["a1c"])
            sq.add("vector", lambda e: e.tensor_scalar_add(
                a1c[:, 8:16], a1c[:, 8:16], 1.0),
                reads=["a1c"], writes=["a1c"])

            def wload(d):
                wb = d % 2
                for wname, wt, stile in (("wq", wq, swq), ("wk", wk, swk),
                                         ("wv", wv, swv)):
                    for c in range(NCH):
                        sq.add("gpsimd", lambda e, wt=wt, stile=stile, d=d,
                               wb=wb, c=c: e.dma_start(
                                   out=stile[:, wb, c, :],
                                   in_=wt[128 * c:128 * (c + 1),
                                          128 * d:128 * (d + 1)]),
                               writes=[f"{wname}:{wb}"], dma=True)

            for t in range(NT):
                ln_stats(t)
            ln_batch()
            for t in range(NT):
                b = t % 2
                ln_apply(t, sy[:, b, :], f"sy:{b}")
                for c in range(NCH):
                    pp = c % 2
                    sq.add("tensor", lambda e, b=b, c=c, pp=pp: e.transpose(
                        p_tr[pp][:], sy[:, b, 128 * c:128 * (c + 1)], sid128[:]),
                        reads=[f"sy:{b}", "sid"], writes=[f"ptr:{pp}"])
                    sq.add("vector", lambda e, t=t, c=c, pp=pp: e.tensor_scalar(
                        shT[:, c, 128 * t:128 * (t + 1)], p_tr[pp][:],
                        a1c[:, 8 + c:9 + c], a1c[:, c:c + 1],
                        ALU.mult, ALU.add),
                        reads=[f"ptr:{pp}", "a1c"], writes=[f"hT:{c}:{t // 4}"])

            for h in range(NH):
                sl = h % 4
                qe = "sync" if h % 2 == 0 else "scalar"
                src5 = AP(dtaus, 384 * h, [[0, 128], [1, 384]])
                sq.add(qe, lambda e, src5=src5, sl=sl: e.dma_start(
                    out=srep4[:, sl, :], in_=src5),
                    reads=["dtaus"], writes=[f"srep:{sl}"], dma=True)
                sq.add(qe, lambda e, h=h, sl=sl: e.dma_start(
                    out=dtau[:, 384 * h:384 * (h + 1)], in_=srep4[:, sl, :]),
                    reads=[f"srep:{sl}"], writes=[f"dtau:{h}"], dma=True)

            mod_group(1)
            mod_group(2)
            wload(0)
            wload(1)

            def load_row(offs, deps):
                sq.add("scalar", lambda e, offs=offs: e.dma_start(
                    out=srow[:], in_=drows[:, offs:offs + D]),
                    reads=list(deps), writes=["srow"], dma=True)

            def bcast_tile(dst, tag, plus1=False):
                for half in range(2):
                    sq.add("tensor", lambda e, half=half: e.matmul(
                        p_gen[:], ones1[:], srowb[:, 512 * half:512 * (half + 1)],
                        start=True, stop=True),
                        reads=["srowb", "ones1"], writes=["p_gen"])
                    bias = F1 if plus1 else F0
                    sq.add("scalar", lambda e, half=half, dst=dst, bias=bias: e.activation(
                        dst[:, 512 * half:512 * (half + 1)], p_gen[:], AF.Identity,
                        bias=bias, scale=F1),
                        reads=["p_gen"], writes=[tag])

            def clamp_gate(offs, deps):
                load_row(offs, deps)
                sq.add("scalar", lambda e: e.activation(srow2[:], srow[:], AF.Abs, bias=F0, scale=F1),
                       reads=["srow"], writes=["srow2"])
                sq.add("vector", lambda e: e.tensor_scalar_max(srow2[:], srow2[:], 0.25),
                       reads=["srow2"], writes=["srow2"])
                sq.add("vector", lambda e: e.tensor_scalar(
                    srow[:], srow[:], 0.0, None, ALU.is_ge),
                    reads=["srow"], writes=["srow"])
                sq.add("vector", lambda e: e.tensor_scalar(
                    srow[:], srow[:], 2.0, -1.0,
                    ALU.mult, ALU.add),
                    reads=["srow"], writes=["srow"])
                sq.add("vector", lambda e: e.tensor_mul(srow2[:], srow2[:], srow[:]),
                       reads=["srow"], writes=["srow2"])

            def save_row(slot):
                sq.add("scalar", lambda e, slot=slot: e.dma_start(
                    out=drow2[:, D * slot:D * (slot + 1)], in_=srowb[:]),
                    reads=["srowb"], writes=[f"dr2:{slot}"], dma=True)

            clamp_gate(2048, ["dr:1:0", "dr:1:1"])
            sq.add("vector", lambda e: e.tensor_copy(srowb[:], srow2[:]),
                   reads=["srow2"], writes=["srowb"])
            bcast_tile(gat, "gat")
            clamp_gate(5120, ["dr:2:2", "dr:2:3"])
            sq.add("vector", lambda e: e.tensor_copy(srowb[:], srow2[:]),
                   reads=["srow2"], writes=["srowb"])
            save_row(0)
            sq.add("scalar", lambda e: e.dma_start(out=srow[:], in_=fbr[:]),
                   writes=["srow"], dma=True)
            sq.add("vector", lambda e: e.tensor_mul(srow2[:], srow2[:], srow[:]),
                   reads=["srow"], writes=["srow2"])
            sq.add("vector", lambda e: e.tensor_copy(srowb[:], srow2[:]),
                   reads=["srow2"], writes=["srowb"])
            save_row(1)
            load_row(4096, ["dr:2:0", "dr:2:1"])
            sq.add("vector", lambda e: e.tensor_copy(srowb[:], srow[:]),
                   reads=["srow"], writes=["srowb"])
            save_row(2)
            load_row(3072, ["dr:1:2", "dr:1:3"])
            sq.add("vector", lambda e: e.tensor_copy(srowb[:], srow[:]),
                   reads=["srow"], writes=["srowb"])
            save_row(3)

            for c in range(NCH):
                sq.add("gpsimd", lambda e, c=c: e.dma_start(
                    out=swo[:, c, :], in_=wo[128 * c:128 * (c + 1), :]),
                    writes=["swo"], dma=True)

        sq.barrier()

        if upto < 2:
            block = ctx.enter_context(nc.Block())
            sq.build(block, sems, serial=serial)
            return nc

        with ExitStack() as a2:
            sq.enabled = (upto >= 2)

            def sb2(name, shape, dtp):
                return a2.enter_context(nc.sbuf_tensor(name, shape, dtp, side="right"))

            def ps2(name, shape):
                return a2.enter_context(nc.psum_tensor(name, shape, F32))

            soT = sb2("soT", [128, NCH, SEQ], BF16)
            sqt = sb2("sqt", [128, SEQ], BF16)
            skt = sb2("skt", [128, SEQ], BF16)
            sv = sb2("sv", [128, 2, NT, 130], BF16)
            sPt = sb2("sPt", [128, 2, 2, 512], BF16)
            sE = sb2("sE", [128, NH, 240], BF16)
            srowsb = [sb2(f"srowsb{i}", [1, 4, 512], BF16) for i in range(2)]
            srs32 = sb2("srs32", [32, 128], F32)
            srs32b = sb2("srs32b", [32, 128], BF16)
            srs32c = sb2("srs32c", [32, 128], BF16)
            srbt = sb2("srbt", [128, 2048], BF16)
            p_s = [ps2(f"p_s{i}", [128, 512]) for i in range(4)]
            p_o2 = [ps2(f"p_o2{i}", [65, 512]) for i in range(2)]
            p_v = [ps2(f"p_v{i}", [128, 128]) for i in range(2)]
            sv4 = sv[:].rearrange("p b t (g c) -> p b t g c", c=65)

            def epi_A():
                sq.add("sync", lambda e: e.dma_start(
                    out=srs32b[:],
                    in_=dsumB2[:].rearrange("h (p j) -> (h p) j", p=16)),
                    reads=["dsB2"], writes=["srs32b"], dma=True)
                sq.add("vector", lambda e: e.reciprocal(srs32[:], srs32b[:]),
                       reads=["srs32b"], writes=["srs32"])
                sq.add("vector", lambda e: e.tensor_copy(srs32c[:], srs32[:]),
                       reads=["srs32"], writes=["srs32c"])
                sq.add("sync", lambda e: e.dma_start(
                    out=dsumB3[:].rearrange("h (p j) -> (h p) j", p=16),
                    in_=srs32c[:]),
                    reads=["srs32c"], writes=["dsB3"], dma=True)

            def epi_B(d0, hb0):
                o0 = 64 * hb0
                src = AP(dsumB3, 2048 * hb0, [[0, 64], [1, 2048]])
                sq.add("sync", lambda e, src=src, o=o0: e.dma_start(
                    out=srbt[o:o + 64, :], in_=src),
                    reads=["dsB3"], writes=[f"srbt:{hb0}"], dma=True)
                for qc in range(NQC):
                    sq.add("vector", lambda e, qc=qc, o=o0, d=d0: e.tensor_mul(
                        soT[o:o + 64, d, 512 * qc:512 * (qc + 1)],
                        soT[o:o + 64, d, 512 * qc:512 * (qc + 1)],
                        srbt[o:o + 64, 512 * qc:512 * (qc + 1)]),
                        reads=[f"srbt:{hb0}", "soT"], writes=["soT"])

            for h in range(NH):
                esrc = AP(dtau, 384 * h + 127, [[NH * 384 - 1, 128], [1, 240]])
                sq.add("sync", lambda e, h=h, esrc=esrc: e.dma_start(
                    out=sE[:, h, :], in_=esrc),
                    reads=[f"dtau:{h}"], writes=[f"E:{h}"], dma=True)

            def qkproj(d):
                wb = d % 2
                for which, wtag, stile, dst, dsttag, bc, btag in (
                        (0, "wq", swq, sqt, "sqt", sbqc, "bqc"),
                        (1, "wk", swk, skt, "skt", sbkc, "bkc")):
                    for qc in range(NQC):
                        pb = p_s[2 * which + (qc % 2)]
                        pt = f"ps:{2 * which + (qc % 2)}"
                        for c in range(NCH):
                            sq.add("tensor", lambda e, stile=stile, wb=wb, c=c,
                                   qc=qc, pb=pb: e.matmul(
                                       pb[:], stile[:, wb, c, :],
                                       shT[:, c, 512 * qc:512 * (qc + 1)],
                                       start=(c == 0), stop=(c == NCH - 1)),
                                   reads=[f"{wtag}:{wb}", f"hT:{c}:{qc}"],
                                   writes=[pt])
                        sq.add("vector", lambda e, dst=dst, qc=qc, pb=pb, bc=bc,
                               d=d, pt=pt: e.tensor_scalar(
                                   dst[:, 512 * qc:512 * (qc + 1)], pb[:],
                                   bc[:, d:d + 1], None, ALU.add),
                               reads=[pt, btag], writes=[dsttag])

            def vproj_steps(d):
                wb = d % 2
                vb = d % 2
                steps = []

                def memset_ones(vb=vb):
                    sq.add("vector", lambda e, vb=vb: e.memset(
                        sv4[:, vb, :, :, 64:65], 1.0),
                        writes=[f"svones:{vb}"])
                steps.append(memset_ones)
                for t in range(NT):
                    def grp(t=t, wb=wb, vb=vb, d=d):
                        pv = p_v[t % 2]
                        pvt = f"pv:{t % 2}"
                        for c in range(NCH):
                            sq.add("tensor", lambda e, wb=wb, c=c, t=t,
                                   pv=pv: e.matmul(
                                pv[:],
                                shT[:, c, 128 * t:128 * (t + 1)],
                                swv[:, wb, c, :], start=(c == 0), stop=False),
                                reads=[f"wv:{wb}", f"hT:{c}:{t // 4}"],
                                writes=[pvt])
                        sq.add("tensor", lambda e, d=d, pv=pv: e.matmul(
                            pv[:], ones1[:],
                            sbvb[:, 128 * d:128 * (d + 1)],
                            start=False, stop=True),
                            reads=["bvb", "ones1"], writes=[pvt])
                        sq.add("vector", lambda e, t=t, vb=vb, pv=pv: e.tensor_copy(
                            sv4[:, vb, t, :, 0:64],
                            pv[:].rearrange("p (g c) -> p g c", c=64)),
                            reads=[pvt], writes=[f"sv:{vb}:{t}"])
                    steps.append(grp)
                return steps

            prev = None
            for d in range(NP):
                vb = d % 2
                qkproj(d)
                if d == 0:
                    for st in vproj_steps(0):
                        st()
                if d + 2 < NP:
                    wload(d + 2)
                vsteps = vproj_steps(d + 1) if d + 1 < NP else []
                vinj = [0]
                step_idx = [0]
                TOT = sum(4 * qc + 4 for qc in range(NQC))

                for qc in range(NQC):
                    nkt = 4 * qc + 4

                    def emit_qk(hb, kt, qc=qc):
                        qoff = max(0, 128 * kt - 512 * qc)
                        bi = (2 * kt + hb) % 4
                        pb = p_s[bi]
                        o = 64 * hb
                        sq.add("tensor", lambda e, kt=kt, qoff=qoff, pb=pb,
                               o=o, qc=qc: e.matmul(
                                   pb[:, qoff:512],
                                   skt[o:o + 64, 128 * kt:128 * (kt + 1)],
                                   sqt[o:o + 64, 512 * qc + qoff:512 * (qc + 1)],
                                   start=True, stop=True, tile_position=(o, 0)),
                               reads=["sqt", "skt"], writes=[f"ps:{bi}"])

                    def emit_sm(hb, kt, qc=qc, d=d):
                        qoff = max(0, 128 * kt - 512 * qc)
                        bi = (2 * kt + hb) % 4
                        pb = p_s[bi]
                        sl = kt % 2
                        h = 2 * d + hb
                        ptt = f"pt:{sl}:{hb}"
                        sq.add("scalar", lambda e, qoff=qoff, pb=pb, sl=sl,
                               hb=hb: e.activation(
                            sPt[:, sl, hb, qoff:512], pb[:, qoff:512],
                            AF.Exp, bias=EXPB, scale=F1),
                            reads=[f"ps:{bi}"], writes=[ptt])
                        j = kt - 4 * qc
                        if j >= 0:
                            w = min(240, 512 - 128 * j)
                            sq.add("vector", lambda e, j=j, w=w, h=h, sl=sl,
                                   hb=hb: e.tensor_mul(
                                sPt[:, sl, hb, 128 * j:128 * j + w],
                                sPt[:, sl, hb, 128 * j:128 * j + w],
                                sE[:, h, 0:w]),
                                reads=[f"E:{h}", ptt], writes=[ptt])
                        elif j == -1:
                            sq.add("vector", lambda e, h=h, sl=sl,
                                   hb=hb: e.tensor_mul(
                                sPt[:, sl, hb, 0:112], sPt[:, sl, hb, 0:112],
                                sE[:, h, 128:240]),
                                reads=[f"E:{h}", ptt], writes=[ptt])

                    def emit_pv(hb, kt, qc=qc, nkt=nkt, vb=vb):
                        qoff = max(0, 128 * kt - 512 * qc)
                        sl = kt % 2
                        sq.add("tensor", lambda e, kt=kt, qoff=qoff, hb=hb,
                               sl=sl, nkt=nkt, vb=vb: e.matmul(
                                   p_o2[hb][:, qoff:512],
                                   sv[:, vb, kt, 65 * hb:65 * hb + 65],
                                   sPt[:, sl, hb, qoff:512],
                                   start=(kt == 0), stop=(kt == nkt - 1)),
                               reads=[f"pt:{sl}:{hb}", f"sv:{vb}:{kt}",
                                      f"svones:{vb}"],
                               writes=[f"po:{hb}"])

                    emit_qk(0, 0)
                    emit_qk(1, 0)
                    for kt in range(nkt):
                        if kt + 1 < nkt:
                            emit_qk(0, kt + 1)
                            emit_qk(1, kt + 1)
                        for hb in (0, 1):
                            emit_sm(hb, kt)
                            emit_pv(hb, kt)
                        step_idx[0] += 1
                        if vsteps:
                            want = (step_idx[0] * len(vsteps)) // TOT
                            while vinj[0] < want:
                                vsteps[vinj[0]]()
                                vinj[0] += 1

                    if prev is not None:
                        if qc == 0:
                            epi_A()
                        elif qc == 1:
                            epi_B(prev, 0)
                        elif qc == 2:
                            epi_B(prev, 1)

                    for hb in (0, 1):
                        sq.add("vector", lambda e, qc=qc, hb=hb, d=d: e.tensor_copy(
                            soT[64 * hb:64 * hb + 64, d,
                                512 * qc:512 * (qc + 1)],
                            p_o2[hb][0:64, :]),
                            reads=[f"po:{hb}"], writes=["soT"])
                        sq.add("scalar", lambda e, qc=qc, hb=hb: e.activation(
                            srowsb[hb][0:1, qc, :], p_o2[hb][64:65, :],
                            AF.Identity, bias=F0, scale=F1),
                            reads=[f"po:{hb}"], writes=[f"srb:{hb}"])

                while vsteps and vinj[0] < len(vsteps):
                    vsteps[vinj[0]]()
                    vinj[0] += 1
                for hb in (0, 1):
                    sq.add("sync", lambda e, hb=hb: e.dma_start(
                        out=dsumB2[hb:hb + 1, :],
                        in_=srowsb[hb][:].rearrange("o q j -> o (q j)")),
                        reads=[f"srb:{hb}"], writes=["dsB2"], dma=True)
                prev = d

            epi_A()
            epi_B(prev, 0)
            epi_B(prev, 1)

            for t in range(NT):
                for half in range(2):
                    pb = p_s[(2 * t + half) % 2]
                    pt = f"ps:{(2 * t + half) % 2}"
                    for c in range(NCH):
                        sq.add("tensor", lambda e, t=t, half=half, c=c, pb=pb: e.matmul(
                            pb[:], soT[:, c, 128 * t:128 * (t + 1)],
                            swo[:, c, 512 * half:512 * (half + 1)],
                            start=(c == 0), stop=False),
                            reads=["soT", "swo"], writes=[pt])
                    sq.add("tensor", lambda e, half=half, pb=pb: e.matmul(
                        pb[:], ones1[:], sbob[:, 512 * half:512 * (half + 1)],
                        start=False, stop=True),
                        reads=["bob", "ones1"], writes=[pt])
                    sq.add("vector", lambda e, half=half, pb=pb, pt=pt: e.scalar_tensor_tensor(
                        p_s[2][:], pb[:], 1.0, gat[:, 512 * half:512 * (half + 1)],
                        ALU.mult, ALU.mult),
                        reads=[pt, "gat"], writes=["ps:2"])
                    sq.add("vector", lambda e, t=t, half=half: e.tensor_add(
                        sx[:, t, 512 * half:512 * (half + 1)],
                        sx[:, t, 512 * half:512 * (half + 1)], p_s[2][:]),
                        reads=["ps:2"], writes=[f"x:{t}"])
                if half == 1:
                    ln_stats(t)

            ln_batch()

        sq.barrier()

        with ExitStack() as a3:
            sq.enabled = (upto >= 3)

            def sb3(name, shape, dtp):
                return a3.enter_context(nc.sbuf_tensor(name, shape, dtp, side="right"))

            def ps3(name, shape):
                return a3.enter_context(nc.psum_tensor(name, shape, F32))

            spk = sb3("spk", [128, NCH, 256], I32)
            swf = sb3("swf", [128, NCH, D], BF16)
            sy3 = sb3("sy3", [128, 2, D], BF16)
            sh2 = sb3("sh2", [128, 2, D], BF16)
            st32 = sb3("st32", [128, 2, D], F32)
            sqi = sb3("sqi", [128, 2, D], I32)
            sei = sb3("sei", [128, 256], I32)
            gbw = sb3("gbw", [128, D], BF16)
            gbiast = sb3("gbiast", [128, D], BF16)
            a2t = sb3("a2t", [128, D], BF16)
            b2t = sb3("b2t", [128, D], BF16)
            srowb3 = sb3("srowb3", [1, D], BF16)
            p_f = [ps3(f"p_f{i}", [128, 512]) for i in range(2)]
            p_g3 = ps3("p_g3", [128, 512])
            p_tr3 = [a3.enter_context(nc.psum_tensor(f"p_tr3{i}", [128, 128], BF16))
                     for i in range(2)]

            def rebuild(dst, tag, slot, plus1=False):
                sq.add("sync", lambda e, slot=slot: e.dma_start(
                    out=srowb3[:], in_=drow2[:, D * slot:D * (slot + 1)]),
                    reads=[f"dr2:{slot}"], writes=["srowb3"], dma=True)
                for half in range(2):
                    sq.add("tensor", lambda e, half=half: e.matmul(
                        p_g3[:], ones1[:], srowb3[:, 512 * half:512 * (half + 1)],
                        start=True, stop=True),
                        reads=["srowb3", "ones1"], writes=["pg3"])
                    bias = F1 if plus1 else F0
                    sq.add("scalar", lambda e, half=half, dst=dst, bias=bias: e.activation(
                        dst[:, 512 * half:512 * (half + 1)], p_g3[:], AF.Identity,
                        bias=bias, scale=F1),
                        reads=["pg3"], writes=[tag])

            rebuild(gbw, "gbw", 0)
            rebuild(gbiast, "gbiast", 1)
            rebuild(a2t, "a2t", 2, plus1=True)
            rebuild(b2t, "b2t", 3)

            for c in range(NCH):
                sq.add("sync", lambda e, c=c: e.dma_start(
                    out=spk[:, c, :], in_=pkt[128 * c:128 * (c + 1), :]),
                    writes=["spk"], dma=True)
            def unpack_chunk(c):
                for s in range(4):
                    sq.add("vector", lambda e, c=c, s=s: e.tensor_scalar(
                        sei[:], spk[:, c, :], 2 * s, 3,
                        ALU.logical_shift_right, ALU.bitwise_and),
                        reads=["spk"], writes=["sei"])
                    sq.add("vector", lambda e, c=c, s=s: e.tensor_scalar(
                        swf[:, c, 256 * s:256 * (s + 1)], sei[:], 1, None,
                        ALU.subtract),
                        reads=["sei"], writes=[f"swf:{c}"])
                sq.add("gpsimd", lambda e, c=c: e.tensor_mul(
                    swf[:, c, :], swf[:, c, :], gbw[:]),
                    reads=["gbw", f"swf:{c}"], writes=[f"swf:{c}"])

            for c in range(NCH):
                unpack_chunk(c)
            def q_stageA(t):
                b = t % 2
                ln_apply(t, sy3[:, b, :], f"sy:{b}")
                sq.add("vector", lambda e, b=b: e.tensor_mul(
                    sh2[:, b, :], sy3[:, b, :], a2t[:]),
                    reads=[f"sy:{b}", "a2t"], writes=[f"sh2:{b}"])
                sq.add("vector", lambda e, b=b: e.tensor_add(
                    sh2[:, b, :], sh2[:, b, :], b2t[:]),
                    reads=["b2t", f"sh2:{b}"], writes=[f"sh2:{b}"])
                sq.add("vector", lambda e, b=b: e.reduce_max(
                    stmp1[:, b:b + 1], sh2[:, b, :], AXX,
                    apply_absolute_value=True),
                    reads=[f"sh2:{b}"], writes=[f"t1:{b}"])
                sq.add("vector", lambda e, b=b: e.tensor_scalar_max(
                    stmp1[:, b:b + 1], stmp1[:, b:b + 1], 1e-5),
                    reads=[f"t1:{b}"], writes=[f"t1:{b}"])
                sq.add("vector", lambda e, t=t, b=b: e.tensor_scalar(
                    skap[:, t:t + 1], stmp1[:, b:b + 1], s_swb[:], 1.0 / 127.0,
                    ALU.mult, ALU.mult),
                    reads=[f"t1:{b}", "swb"], writes=[f"kap:{t}"])
                sq.add("vector", lambda e, b=b: e.reciprocal(
                    stmp1b[:, b:b + 1], stmp1[:, b:b + 1]),
                    reads=[f"t1:{b}"], writes=[f"t1b:{b}"])
                sq.add("vector", lambda e, t=t, b=b: e.tensor_scalar_mul(
                    ssc[:, t:t + 1], stmp1b[:, b:b + 1], 127.0),
                    reads=[f"t1b:{b}"], writes=[f"qsc:{t}"])

            def q_stageB(t):
                b = t % 2
                sq.add("scalar", lambda e, t=t, b=b: e.activation(
                    st32[:, b, :], sh2[:, b, :], AF.Identity,
                    scale=ssc[:, t:t + 1], bias=F0),
                    reads=[f"sh2:{b}", f"qsc:{t}"], writes=[f"st32:{b}"])
                sq.add("vector", lambda e, b=b: e.tensor_copy(
                    sqi[:, b, :], st32[:, b, :]),
                    reads=[f"st32:{b}"], writes=[f"sqi:{b}"])
                sq.add("vector", lambda e, b=b: e.tensor_copy(
                    sy3[:, b, :], sqi[:, b, :]),
                    reads=[f"sqi:{b}"], writes=[f"sy:{b}"])
                for c in range(NCH):
                    pp = c % 2
                    sq.add("tensor", lambda e, b=b, c=c, pp=pp: e.transpose(
                        p_tr3[pp][:], sy3[:, b, 128 * c:128 * (c + 1)], sid128[:]),
                        reads=[f"sy:{b}", "sid"], writes=[f"ptr:{pp}"])
                    if c % 2 == 0:
                        sq.add("scalar", lambda e, t=t, c=c, pp=pp: e.activation(
                            shT[:, c, 128 * t:128 * (t + 1)], p_tr3[pp][:],
                            AF.Identity, bias=F0, scale=F1),
                            reads=[f"ptr:{pp}"], writes=[f"hT:{c}:{t // 4}"])
                    else:
                        sq.add("vector", lambda e, t=t, c=c, pp=pp: e.tensor_copy(
                            shT[:, c, 128 * t:128 * (t + 1)], p_tr3[pp][:]),
                            reads=[f"ptr:{pp}"], writes=[f"hT:{c}:{t // 4}"])

            q_stageA(0)
            for t in range(NT):
                if t + 1 < NT:
                    q_stageA(t + 1)
                q_stageB(t)

            for t in range(NT):
                for half in range(2):
                    pb = p_f[(2 * t + half) % 2]
                    pt = f"pf:{(2 * t + half) % 2}"
                    for c in range(NCH):
                        sq.add("tensor", lambda e, t=t, half=half, c=c, pb=pb: e.matmul(
                            pb[:], shT[:, c, 128 * t:128 * (t + 1)],
                            swf[:, c, 512 * half:512 * (half + 1)],
                            start=(c == 0), stop=(c == NCH - 1)),
                            reads=[f"hT:{c}:{t // 4}", f"swf:{c}"], writes=[pt]),
                    sq.add("vector", lambda e, t=t, half=half, pb=pb, pt=pt: e.scalar_tensor_tensor(
                        p_g3[:], pb[:], skap[:, t:t + 1],
                        gbiast[:, 512 * half:512 * (half + 1)],
                        ALU.mult, ALU.add),
                        reads=[pt, f"kap:{t}", "gbiast"], writes=["pg3"])
                    sq.add("vector", lambda e, t=t, half=half: e.tensor_add(
                        sx[:, t, 512 * half:512 * (half + 1)],
                        sx[:, t, 512 * half:512 * (half + 1)], p_g3[:]),
                        reads=["pg3"], writes=[f"x:{t}"])

                sq.add("sync", lambda e, t=t: e.dma_start(
                    out=out[128 * t:128 * (t + 1), :], in_=sx[:, t, :]),
                    reads=[f"x:{t}"], writes=[f"out:{t}"], dma=True)
            sq.add("vector", lambda e: e.engine_nop(),
                   reads=[f"out:{t}" for t in range(NT)], writes=["done"])

        block = ctx.enter_context(nc.Block())
        sq.build(block, sems, serial=serial)
    return nc


_CACHED = {}


def _get_nc():
    if "nc" not in _CACHED:
        _CACHED["nc"] = build_nc()
    return _CACHED["nc"]


def kernel(L0_attn_x, L0_attn_Wq, L0_attn_bq, L0_attn_Wk, L0_attn_bk,
           L0_attn_Wv, L0_attn_bv, L0_attn_Wo, L0_attn_bo, L0_attn_rel_emb,
           L0_ffn_packed_weight, L0_ffn_weight_scale, L0_ffn_bias, L0_c,
           L0_wmod):
    x = np.ascontiguousarray(np.asarray(L0_attn_x, np.float32))
    wqT = np.ascontiguousarray(np.asarray(L0_attn_Wq, np.float32).T)
    wkT = np.ascontiguousarray(np.asarray(L0_attn_Wk, np.float32).T)
    wvT = np.ascontiguousarray(np.asarray(L0_attn_Wv, np.float32).T)
    woT = np.ascontiguousarray(np.asarray(L0_attn_Wo, np.float32).T)
    wmv = np.ascontiguousarray(np.asarray(L0_wmod, np.float32))
    pkT = np.ascontiguousarray(
        np.asarray(L0_ffn_packed_weight, np.int32).reshape(256, 1024).T)
    relv = np.ascontiguousarray(np.asarray(L0_attn_rel_emb, np.float32))
    ohv = _oh_const()
    c = np.asarray(L0_c, np.float32)
    bqcv = np.ascontiguousarray(np.asarray(L0_attn_bq, np.float32).reshape(8, 128).T)
    bkcv = np.ascontiguousarray(np.asarray(L0_attn_bk, np.float32).reshape(8, 128).T)
    bvr = np.ascontiguousarray(np.asarray(L0_attn_bv, np.float32).reshape(1, D))
    borv = np.ascontiguousarray(np.asarray(L0_attn_bo, np.float32).reshape(1, D))
    fbrv = np.ascontiguousarray(np.asarray(L0_ffn_bias, np.float32).reshape(1, D))
    wscv = np.asarray(L0_ffn_weight_scale, np.float32).reshape(1, 1)
    eye16 = np.eye(16, dtype=np.float32)
    eye128 = np.eye(128, dtype=ml_dtypes.bfloat16)

    nc = _get_nc()
    in_maps = []
    for b in range(8):
        in_maps.append({
            "x": x[b], "wq": wqT, "wk": wkT, "wv": wvT, "wo": woT,
            "wm": wmv, "pk": pkT,
            "ccol": np.ascontiguousarray(c[b].reshape(8, 128).T),
            "bqc": bqcv, "bkc": bkcv, "bvr": bvr, "bor": borv, "fbr": fbrv,
            "rel": relv, "oh": ohv, "wsc": wscv, "idf": eye16,
            "eye128": eye128,
        })
    res = run_bass_kernel_spmd(nc, in_maps, core_ids=list(range(8)))
    return np.stack([r["out"] for r in res.results], axis=0)


# version 22
# speedup vs baseline: 1.0645x; 1.0645x over previous
import math
from contextlib import ExitStack

import ml_dtypes
import numpy as np

import concourse.bass as bass
import concourse.mybir as mybir
from concourse.ap import AP
from concourse.bass_utils import run_bass_kernel_spmd

F32 = mybir.dt.float32
F32R = mybir.dt.float32r
BF16 = mybir.dt.bfloat16
I32 = mybir.dt.int32
AF = mybir.ActivationFunctionType
ALU = mybir.AluOpType
AXX = mybir.AxisListType.X

SEQ = 2048
D = 1024
NH = 16
NT = SEQ // 128
NCH = D // 128
NP = NH // 2
NQC = SEQ // 512
EXPB = np.float32(-12.0)
F1 = np.float32(1.0)
F0 = np.float32(0.0)


class Sched:
    ENGINES = ("tensor", "vector", "scalar", "gpsimd", "sync")
    NDMA = 8
    BARRIER_DMA = ("sync", "scalar")

    def __init__(self):
        self.steps = []

    enabled = True

    def add(self, eng, fn, reads=(), writes=(), dma=False):
        if not self.enabled:
            return
        self.steps.append([eng, fn, tuple(reads), tuple(writes), bool(dma)])

    def barrier(self):
        self.steps.append(["BARRIER", None, (), (), False])

    def build(self, block, sems, serial=False):
        steps = self.steps
        n = len(steps)
        cpos, dpos = {}, {}
        ccount = {e: 0 for e in self.ENGINES}
        dcount = {e: 0 for e in self.ENGINES}
        deps = [set() for _ in range(n)]
        last_w, readers = {}, {}
        dqueues = {}
        bar_after = [None] * n
        snap = None
        for i, (eng, fn, rd, wr, dma) in enumerate(steps):
            if eng == "BARRIER":
                snap = (dict(ccount), dict(dcount))
                continue
            bar_after[i] = snap
            if dma:
                dpos[i] = dcount[eng]
                dcount[eng] += 1
            else:
                cpos[i] = ccount[eng]
                ccount[eng] += 1
            d = deps[i]
            for r in rd:
                j = last_w.get(r)
                if j is not None:
                    d.add(j)
            for w in wr:
                j = last_w.get(w)
                if j is not None:
                    d.add(j)
                for j in readers.get(w, ()):
                    d.add(j)
            if dma:
                q = dqueues.setdefault(eng, [])
                if len(q) >= self.NDMA:
                    d.add(q[-self.NDMA])
                q.append(i)
            for w in wr:
                last_w[w] = i
                readers[w] = []
            for r in rd:
                readers.setdefault(r, []).append(i)

        sched = self

        def mk(engine_name):
            def emit(eng):
                watermark = {}

                def wait_val(key, sem, val):
                    if watermark.get(key, -1) >= val:
                        return
                    watermark[key] = val
                    eng.wait_ge(sem, val)

                def wait_step(j):
                    e2, _, _, _, dm2 = steps[j]
                    if dm2:
                        k = dpos[j]
                        key = f"d:{e2}:{k % sched.NDMA}"
                        wait_val(key, sems[key], 16 * (k // sched.NDMA + 1))
                    elif e2 != engine_name or e2 != "tensor":
                        key = f"c:{e2}"
                        wait_val(key, sems[key], cpos[j] + 1)

                emitted_bar = None
                run_cc = {e2: 0 for e2 in sched.ENGINES}
                run_dc = {e2: 0 for e2 in sched.ENGINES}
                for i, (e, fn, rd, wr, dma) in enumerate(steps):
                    if e == "BARRIER":
                        continue
                    if serial and e != engine_name:
                        if dma:
                            run_dc[e] += 1
                        else:
                            run_cc[e] += 1
                        continue
                    if serial:
                        for e2 in sched.ENGINES:
                            if run_cc[e2] > 0:
                                wait_val(f"c:{e2}", sems[f"c:{e2}"], run_cc[e2])
                            for kk in range(max(0, run_dc[e2] - sched.NDMA), run_dc[e2]):
                                key = f"d:{e2}:{kk % sched.NDMA}"
                                wait_val(key, sems[key],
                                         16 * (kk // sched.NDMA + 1))
                        if dma:
                            run_dc[e] += 1
                        else:
                            run_cc[e] += 1
                        ins = fn(eng)
                        if dma:
                            k = dpos[i]
                            ins.then_inc(sems[f"d:{engine_name}:{k % sched.NDMA}"], 16)
                        else:
                            ins.then_inc(sems[f"c:{engine_name}"], 1)
                        continue
                    if e != engine_name:
                        continue
                    if bar_after[i] is not None and bar_after[i] is not emitted_bar:
                        cc, dc = bar_after[i]
                        emitted_bar = bar_after[i]
                        for e2 in sched.ENGINES:
                            if e2 != engine_name and cc[e2] > 0:
                                wait_val(f"c:{e2}", sems[f"c:{e2}"], cc[e2])
                        for e2 in sched.BARRIER_DMA:
                            for kk in range(max(0, dc[e2] - sched.NDMA), dc[e2]):
                                key = f"d:{e2}:{kk % sched.NDMA}"
                                wait_val(key, sems[key],
                                         16 * (kk // sched.NDMA + 1))
                    for j in sorted(deps[i]):
                        wait_step(j)
                    ins = fn(eng)
                    if dma:
                        k = dpos[i]
                        ins.then_inc(sems[f"d:{engine_name}:{k % sched.NDMA}"], 16)
                    else:
                        ins.then_inc(sems[f"c:{engine_name}"], 1)
            return emit

        block.tensor(mk("tensor"))
        block.vector(mk("vector"))
        block.scalar(mk("scalar"))
        block.gpsimd(mk("gpsimd"))
        block.sync(mk("sync"))


def _oh_const():
    d = np.arange(128)
    t = np.log(np.maximum(d, 1).astype(np.float32) / np.float32(16.0))
    t = (t / np.float32(math.log(8.0))).astype(np.float32)
    t = (t * np.float32(16.0)).astype(np.float32)
    vl = np.minimum(16 + t.astype(np.int32), 31)
    bucket = np.where(d < 16, d, vl)
    oh = np.zeros((32, 128), np.float32)
    scale = np.float32((D // NH) ** (-0.5))
    for dlt in range(128):
        oh[bucket[dlt], dlt] = scale
    return oh


def build_nc(serial=False, upto=3):
    nc = bass.Bass("TRN2", target_bir_lowering=False)
    dt = nc.dram_tensor
    x_in = dt("x", [SEQ, D], F32, kind="ExternalInput")
    wq = dt("wq", [D, D], F32, kind="ExternalInput")
    wk = dt("wk", [D, D], F32, kind="ExternalInput")
    wv = dt("wv", [D, D], F32, kind="ExternalInput")
    wo = dt("wo", [D, D], F32, kind="ExternalInput")
    wm = dt("wm", [D, 6 * D], F32, kind="ExternalInput")
    pkt = dt("pk", [D, 256], I32, kind="ExternalInput")
    ccol = dt("ccol", [128, 8], F32, kind="ExternalInput")
    bqc_d = dt("bqc", [128, 8], F32, kind="ExternalInput")
    bkc_d = dt("bkc", [128, 8], F32, kind="ExternalInput")
    bvr = dt("bvr", [1, D], F32, kind="ExternalInput")
    bor = dt("bor", [1, D], F32, kind="ExternalInput")
    fbr = dt("fbr", [1, D], F32, kind="ExternalInput")
    rel = dt("rel", [32, 16], F32, kind="ExternalInput")
    ohd = dt("oh", [32, 128], F32, kind="ExternalInput")
    wsc = dt("wsc", [1, 1], F32, kind="ExternalInput")
    idf = dt("idf", [16, 16], F32, kind="ExternalInput")
    eyed = dt("eye128", [128, 128], BF16, kind="ExternalInput")
    out = dt("out", [SEQ, D], F32, kind="ExternalOutput")
    drows = dt("drows", [1, 8 * D], F32, kind="Internal")
    dsumB2 = dt("dsumB2", [2, 2048], BF16, kind="Internal")
    dsumB3 = dt("dsumB3", [2, 2048], BF16, kind="Internal")
    drow2 = dt("drow2", [1, 4 * D], BF16, kind="Internal")
    dtaus = dt("dtaus", [16, 384], BF16, kind="Internal")
    dtau = dt("dtau", [128, NH * 384], BF16, kind="Internal")

    sq = Sched()
    with ExitStack() as ctx:
        def sb(name, shape, dtp, **kw):
            return ctx.enter_context(nc.sbuf_tensor(name, shape, dtp, **kw))

        sx = sb("sx", [128, NT, D], F32)
        shT = sb("shT", [128, NCH, SEQ], BF16)
        gat = sb("gat", [128, D], BF16)
        swq = sb("swq", [128, 2, NCH, 128], BF16)
        swk = sb("swk", [128, 2, NCH, 128], BF16)
        swv = sb("swv", [128, 2, NCH, 128], BF16)
        swo = sb("swo", [128, NCH, D], BF16)
        sid128 = sb("sid128", [128, 128], BF16)
        sccol = sb("sccol", [128, 8], F32)
        ssil = sb("ssil", [128, 8], F32)
        scsil = sb("scsil", [128, 8], F32R)
        sbqc = sb("sbqc", [128, 8], F32)
        sbkc = sb("sbkc", [128, 8], F32)
        a1c = sb("a1c", [128, 16], F32)
        s_swb = sb("s_swb", [128, 1], F32)
        sbvb = sb("sbvb", [1, D], BF16)
        sbob = sb("sbob", [1, D], BF16)
        swscb = sb("swscb", [1, 1], BF16)
        ones1 = sb("ones1", [1, 128], BF16)
        stats6b = sb("stats6b", [128, NT, 2, 6], F32)
        stats2b = sb("stats2b", [128, NT, 2], F32)
        svar16 = sb("svar16", [128, NT], F32)
        srcol = sb("srcol", [128, NT], F32)
        snmr = sb("snmr", [128, NT], F32)
        skap = sb("skap", [128, NT], F32)
        ssc = sb("ssc", [128, NT], F32)
        stmp1 = sb("stmp1", [128, 2], F32)
        stmp1b = sb("stmp1b", [128, 2], F32)

        sems = {}
        for e in Sched.ENGINES:
            sems[f"c:{e}"] = ctx.enter_context(nc.semaphore(f"c_{e}"))
        for e in ("sync", "gpsimd", "scalar"):
            for k in range(Sched.NDMA):
                sems[f"d:{e}:{k}"] = ctx.enter_context(nc.semaphore(f"d_{e}_{k}"))

        XT = [f"x:{t}" for t in range(NT)]

        def ln_stats(t):
            sq.add("vector", lambda e, t=t: e.bn_stats(
                stats6b[:, t, 0, :], sx[:, t, 0:512]),
                reads=[f"x:{t}"], writes=[f"st6:{t}"])
            sq.add("vector", lambda e, t=t: e.bn_stats(
                stats6b[:, t, 1, :], sx[:, t, 512:1024]),
                reads=[f"x:{t}"], writes=[f"st6:{t}"])
            sq.add("vector", lambda e, t=t: e.bn_aggr(
                stats2b[:, t, :], stats6b[:, t, :, :]),
                reads=[f"st6:{t}"], writes=[f"st2:{t}"])

        def ln_batch():
            st2 = [f"st2:{t}" for t in range(NT)]
            sq.add("vector", lambda e: e.tensor_scalar_add(
                svar16[:], stats2b[:, :, 1], 1e-5),
                reads=st2, writes=["svar"])
            sq.add("scalar", lambda e: e.activation(
                svar16[:], svar16[:], AF.Sqrt, bias=F0, scale=F1),
                reads=["svar"], writes=["svar"])
            sq.add("vector", lambda e: e.reciprocal(srcol[:], svar16[:]),
                   reads=["svar"], writes=["r"])
            sq.add("vector", lambda e: e.tensor_mul(
                snmr[:], stats2b[:, :, 0], srcol[:]),
                reads=st2 + ["r"], writes=["nmr"])
            sq.add("vector", lambda e: e.tensor_scalar_mul(snmr[:], snmr[:], -1.0),
                   reads=["nmr"], writes=["nmr"])

        def ln_apply(t, sytile, sytag):
            sq.add("scalar", lambda e, t=t, sytile=sytile: e.activation(
                sytile, sx[:, t, :], AF.Identity,
                scale=srcol[:, t:t + 1], bias=snmr[:, t:t + 1]),
                reads=[f"x:{t}", "r", "nmr"], writes=[sytag])

        with ExitStack() as a1:
            def sb1(name, shape, dtp):
                return a1.enter_context(nc.sbuf_tensor(name, shape, dtp, side="right"))

            def ps1(name, shape):
                return a1.enter_context(nc.psum_tensor(name, shape, F32))

            swmB = sb1("swmB", [128, 2, 2048], BF16)
            swmF = sb1("swmF", [128, 2, 2048], F32)
            scsilb = sb1("scsilb", [128, 8], BF16)
            sy = sb1("sy", [128, 2, D], BF16)
            srow = sb1("srow", [1, D], F32)
            srow2 = sb1("srow2", [1, D], F32)
            srowb = sb1("srowb", [1, D], BF16)
            srel = sb1("srel", [32, 16], F32R)
            soh = sb1("soh", [32, 128], F32R)
            st5 = sb1("st5", [16, 128], F32)
            stau = sb1("stau", [16, 384], F32)
            staub = sb1("staub", [16, 384], BF16)
            srep4 = sb1("srep4", [128, 4, 384], BF16)
            s16 = sb1("s16", [16, 128], F32)
            sidf = sb1("sidf", [16, 16], F32)
            srowst = sb1("srowst", [1, 2, 512], F32)
            p_mod = ps1("p_mod", [1, 2048])
            p_gen = ps1("p_gen", [128, 512])
            p_c16 = ps1("p_c16", [128, 16])
            p_tr = [a1.enter_context(nc.psum_tensor(f"p_tr{i}", [128, 128], BF16))
                    for i in range(2)]

            for t in range(NT):
                sq.add("sync", lambda e, t=t: e.dma_start(
                    out=sx[:, t, :], in_=x_in[128 * t:128 * (t + 1), :]),
                    writes=[f"x:{t}"], dma=True)
            sq.add("sync", lambda e: e.dma_start(out=sccol[:], in_=ccol[:]),
                   writes=["ccol"], dma=True)
            sq.add("sync", lambda e: e.dma_start(out=sbqc[:], in_=bqc_d[:]),
                   writes=["bqc"], dma=True)
            sq.add("sync", lambda e: e.dma_start(out=sbkc[:], in_=bkc_d[:]),
                   writes=["bkc"], dma=True)
            sq.add("sync", lambda e: e.dma_start(out=srow2[:], in_=bvr[:]),
                   writes=["srow2"], dma=True)
            sq.add("vector", lambda e: e.tensor_copy(sbvb[:], srow2[:]),
                   reads=["srow2"], writes=["bvb"])
            sq.add("sync", lambda e: e.dma_start(out=srow[:], in_=bor[:]),
                   writes=["srow"], dma=True)
            sq.add("vector", lambda e: e.tensor_copy(sbob[:], srow[:]),
                   reads=["srow"], writes=["bob"])
            sq.add("gpsimd", lambda e: e.dma_start(out=srel[:], in_=rel[:]),
                   writes=["rel"], dma=True)
            sq.add("gpsimd", lambda e: e.dma_start(out=soh[:], in_=ohd[:]),
                   writes=["oh"], dma=True)
            sq.add("sync", lambda e: e.dma_start(out=sidf[:], in_=idf[:]),
                   writes=["idf"], dma=True)
            sq.add("sync", lambda e: e.dma_start(out=sid128[:], in_=eyed[:]),
                   writes=["sid"], dma=True)
            sq.add("vector", lambda e: e.memset(ones1[:], 1.0), writes=["ones1"])
            sq.add("sync", lambda e: e.dma_start(out=stmp1[0:1, 0:1], in_=wsc[:]),
                   writes=["wscrow"], dma=True)
            sq.add("vector", lambda e: e.tensor_copy(swscb[:], stmp1[0:1, 0:1]),
                   reads=["wscrow"], writes=["wscb"])
            sq.add("tensor", lambda e: e.matmul(
                p_gen[:, 0:1], ones1[:], swscb[:], start=True, stop=True),
                reads=["wscb", "ones1"], writes=["p_gen"])
            sq.add("vector", lambda e: e.tensor_copy(s_swb[:], p_gen[:, 0:1]),
                   reads=["p_gen"], writes=["swb"])

            sq.add("scalar", lambda e: e.activation(ssil[:], sccol[:], AF.Silu, bias=F0, scale=F1),
                   reads=["ccol"], writes=["sil"])
            sq.add("vector", lambda e: e.tensor_copy(scsil[:], ssil[:]),
                   reads=["sil"], writes=["csil"])
            sq.add("vector", lambda e: e.tensor_copy(scsilb[:], ssil[:]),
                   reads=["sil"], writes=["csilb"])

            sq.add("tensor", lambda e: e.matmul(p_gen[0:16, 0:128], srel[:], soh[:],
                                                start=True, stop=True),
                   reads=["rel", "oh"], writes=["p_gen"])
            sq.add("vector", lambda e: e.tensor_copy(st5[:], p_gen[0:16, 0:128]),
                   reads=["p_gen"], writes=["t5"])
            sq.add("vector", lambda e: e.tensor_scalar(
                st5[:], st5[:], st5[:, 127:128], None, ALU.subtract),
                reads=["t5"], writes=["t5"])
            sq.add("vector", lambda e: e.memset(stau[:, 0:127], -1e30), writes=["tau"])
            sq.add("vector", lambda e: e.memset(stau[:, 240:384], 0.0), writes=["tau"])
            sq.add("vector", lambda e: e.tensor_copy(stau[:, 127:240], st5[:, 0:113]),
                   reads=["t5"], writes=["tau"])
            sq.add("scalar", lambda e: e.activation(stau[:], stau[:], AF.Exp, bias=F0, scale=F1),
                   reads=["tau"], writes=["tau"])
            sq.add("vector", lambda e: e.tensor_copy(staub[:], stau[:]),
                   reads=["tau"], writes=["taub"])
            sq.add("sync", lambda e: e.dma_start(out=dtaus[:], in_=staub[:]),
                   reads=["taub"], writes=["dtaus"], dma=True)

            DR = []

            def mod_group(jg):
                qe = ("scalar", "sync", "scalar")[jg]
                for c in range(NCH):
                    p = (NCH * jg + c) % 2
                    sq.add(qe, lambda e, jg=jg, c=c, p=p: e.dma_start(
                        out=swmF[:, p, :],
                        in_=wm[128 * c:128 * (c + 1), 2048 * jg:2048 * (jg + 1)]),
                        writes=[f"swmF:{p}"], dma=True)
                    sq.add("vector", lambda e, p=p: e.tensor_copy(
                        swmB[:, p, :], swmF[:, p, :]),
                        reads=[f"swmF:{p}"], writes=[f"swmB:{p}"])
                    for sl in range(4):
                        sq.add("tensor", lambda e, c=c, sl=sl, p=p: e.matmul(
                            p_mod[:, 512 * sl:512 * (sl + 1)],
                            scsilb[:, c:c + 1],
                            swmB[:, p, 512 * sl:512 * (sl + 1)],
                            start=(c == 0), stop=(c == NCH - 1)),
                            reads=["csilb", f"swmB:{p}"],
                            writes=[f"p_mod:{sl}"])
                for sl in range(4):
                    sq.add("scalar", lambda e, sl=sl: e.activation(
                        srowst[0:1, sl % 2, :], p_mod[0:1, 512 * sl:512 * (sl + 1)],
                        AF.Identity, bias=F0, scale=F1),
                        reads=[f"p_mod:{sl}"], writes=[f"srowst:{sl % 2}"])
                    sq.add("scalar", lambda e, jg=jg, sl=sl: e.dma_start(
                        out=drows[:, 2048 * jg + 512 * sl:2048 * jg + 512 * (sl + 1)],
                        in_=srowst[0:1, sl % 2, :]),
                        reads=[f"srowst:{sl % 2}"], writes=[f"dr:{jg}:{sl}"], dma=True)
                    DR.append(f"dr:{jg}:{sl}")

            mod_group(0)
            sq.add("scalar", lambda e: e.dma_start(
                out=s16[:],
                in_=drows[:, 0:2048].rearrange("o (r j) -> (o r) j", j=128)),
                reads=DR[0:4], writes=["s16"], dma=True)
            sq.add("tensor", lambda e: e.transpose(p_c16[:], s16[:], sidf[:]),
                   reads=["s16", "idf"], writes=["p_c16"])
            sq.add("vector", lambda e: e.tensor_copy(a1c[:], p_c16[:]),
                   reads=["p_c16"], writes=["a1c"])
            sq.add("vector", lambda e: e.tensor_scalar_add(
                a1c[:, 8:16], a1c[:, 8:16], 1.0),
                reads=["a1c"], writes=["a1c"])

            def wload(d):
                wb = d % 2
                for wname, wt, stile in (("wq", wq, swq), ("wk", wk, swk),
                                         ("wv", wv, swv)):
                    for c in range(NCH):
                        sq.add("gpsimd", lambda e, wt=wt, stile=stile, d=d,
                               wb=wb, c=c: e.dma_start(
                                   out=stile[:, wb, c, :],
                                   in_=wt[128 * c:128 * (c + 1),
                                          128 * d:128 * (d + 1)]),
                               writes=[f"{wname}:{wb}"], dma=True)

            wload(0)
            wload(1)

            for t in range(NT):
                ln_stats(t)
            ln_batch()
            for t in range(NT):
                b = t % 2
                ln_apply(t, sy[:, b, :], f"sy:{b}")
                for c in range(NCH):
                    pp = c % 2
                    sq.add("tensor", lambda e, b=b, c=c, pp=pp: e.transpose(
                        p_tr[pp][:], sy[:, b, 128 * c:128 * (c + 1)], sid128[:]),
                        reads=[f"sy:{b}", "sid"], writes=[f"ptr:{pp}"])
                    sq.add("vector", lambda e, t=t, c=c, pp=pp: e.tensor_scalar(
                        shT[:, c, 128 * t:128 * (t + 1)], p_tr[pp][:],
                        a1c[:, 8 + c:9 + c], a1c[:, c:c + 1],
                        ALU.mult, ALU.add),
                        reads=[f"ptr:{pp}", "a1c"], writes=[f"hT:{c}:{t // 4}"])

            for h in range(NH):
                sl = h % 4
                qe = "sync" if h % 2 == 0 else "scalar"
                src5 = AP(dtaus, 384 * h, [[0, 128], [1, 384]])
                sq.add(qe, lambda e, src5=src5, sl=sl: e.dma_start(
                    out=srep4[:, sl, :], in_=src5),
                    reads=["dtaus"], writes=[f"srep:{sl}"], dma=True)
                sq.add(qe, lambda e, h=h, sl=sl: e.dma_start(
                    out=dtau[:, 384 * h:384 * (h + 1)], in_=srep4[:, sl, :]),
                    reads=[f"srep:{sl}"], writes=[f"dtau:{h}"], dma=True)

            mod_group(1)
            mod_group(2)

            def load_row(offs, deps):
                sq.add("scalar", lambda e, offs=offs: e.dma_start(
                    out=srow[:], in_=drows[:, offs:offs + D]),
                    reads=list(deps), writes=["srow"], dma=True)

            def bcast_tile(dst, tag, plus1=False):
                for half in range(2):
                    sq.add("tensor", lambda e, half=half: e.matmul(
                        p_gen[:], ones1[:], srowb[:, 512 * half:512 * (half + 1)],
                        start=True, stop=True),
                        reads=["srowb", "ones1"], writes=["p_gen"])
                    bias = F1 if plus1 else F0
                    sq.add("scalar", lambda e, half=half, dst=dst, bias=bias: e.activation(
                        dst[:, 512 * half:512 * (half + 1)], p_gen[:], AF.Identity,
                        bias=bias, scale=F1),
                        reads=["p_gen"], writes=[tag])

            def clamp_gate(offs, deps):
                load_row(offs, deps)
                sq.add("scalar", lambda e: e.activation(srow2[:], srow[:], AF.Abs, bias=F0, scale=F1),
                       reads=["srow"], writes=["srow2"])
                sq.add("vector", lambda e: e.tensor_scalar_max(srow2[:], srow2[:], 0.25),
                       reads=["srow2"], writes=["srow2"])
                sq.add("vector", lambda e: e.tensor_scalar(
                    srow[:], srow[:], 0.0, None, ALU.is_ge),
                    reads=["srow"], writes=["srow"])
                sq.add("vector", lambda e: e.tensor_scalar(
                    srow[:], srow[:], 2.0, -1.0,
                    ALU.mult, ALU.add),
                    reads=["srow"], writes=["srow"])
                sq.add("vector", lambda e: e.tensor_mul(srow2[:], srow2[:], srow[:]),
                       reads=["srow"], writes=["srow2"])

            def save_row(slot):
                sq.add("scalar", lambda e, slot=slot: e.dma_start(
                    out=drow2[:, D * slot:D * (slot + 1)], in_=srowb[:]),
                    reads=["srowb"], writes=[f"dr2:{slot}"], dma=True)

            clamp_gate(2048, ["dr:1:0", "dr:1:1"])
            sq.add("vector", lambda e: e.tensor_copy(srowb[:], srow2[:]),
                   reads=["srow2"], writes=["srowb"])
            bcast_tile(gat, "gat")
            clamp_gate(5120, ["dr:2:2", "dr:2:3"])
            sq.add("vector", lambda e: e.tensor_copy(srowb[:], srow2[:]),
                   reads=["srow2"], writes=["srowb"])
            save_row(0)
            sq.add("scalar", lambda e: e.dma_start(out=srow[:], in_=fbr[:]),
                   writes=["srow"], dma=True)
            sq.add("vector", lambda e: e.tensor_mul(srow2[:], srow2[:], srow[:]),
                   reads=["srow"], writes=["srow2"])
            sq.add("vector", lambda e: e.tensor_copy(srowb[:], srow2[:]),
                   reads=["srow2"], writes=["srowb"])
            save_row(1)
            load_row(4096, ["dr:2:0", "dr:2:1"])
            sq.add("vector", lambda e: e.tensor_copy(srowb[:], srow[:]),
                   reads=["srow"], writes=["srowb"])
            save_row(2)
            load_row(3072, ["dr:1:2", "dr:1:3"])
            sq.add("vector", lambda e: e.tensor_copy(srowb[:], srow[:]),
                   reads=["srow"], writes=["srowb"])
            save_row(3)

            for c in range(NCH):
                sq.add("gpsimd", lambda e, c=c: e.dma_start(
                    out=swo[:, c, :], in_=wo[128 * c:128 * (c + 1), :]),
                    writes=["swo"], dma=True)

        sq.barrier()

        if upto < 2:
            block = ctx.enter_context(nc.Block())
            sq.build(block, sems, serial=serial)
            return nc

        with ExitStack() as a2:
            sq.enabled = (upto >= 2)

            def sb2(name, shape, dtp):
                return a2.enter_context(nc.sbuf_tensor(name, shape, dtp, side="right"))

            def ps2(name, shape):
                return a2.enter_context(nc.psum_tensor(name, shape, F32))

            soT = sb2("soT", [128, NCH, SEQ], BF16)
            sqt = sb2("sqt", [128, SEQ], BF16)
            skt = sb2("skt", [128, SEQ], BF16)
            sv = sb2("sv", [128, 2, NT, 130], BF16)
            sPt = sb2("sPt", [128, 2, 2, 512], BF16)
            sE = sb2("sE", [128, NH, 240], BF16)
            srowsb = [sb2(f"srowsb{i}", [1, 4, 512], BF16) for i in range(2)]
            srs32 = sb2("srs32", [32, 128], F32)
            srs32b = sb2("srs32b", [32, 128], BF16)
            srs32c = sb2("srs32c", [32, 128], BF16)
            srbt = sb2("srbt", [128, 2048], BF16)
            p_s = [ps2(f"p_s{i}", [128, 512]) for i in range(4)]
            p_o2 = [ps2(f"p_o2{i}", [65, 512]) for i in range(2)]
            p_v = [ps2(f"p_v{i}", [128, 128]) for i in range(2)]
            sv4 = sv[:].rearrange("p b t (g c) -> p b t g c", c=65)

            def epi_A():
                sq.add("sync", lambda e: e.dma_start(
                    out=srs32b[:],
                    in_=dsumB2[:].rearrange("h (p j) -> (h p) j", p=16)),
                    reads=["dsB2"], writes=["srs32b"], dma=True)
                sq.add("vector", lambda e: e.reciprocal(srs32[:], srs32b[:]),
                       reads=["srs32b"], writes=["srs32"])
                sq.add("vector", lambda e: e.tensor_copy(srs32c[:], srs32[:]),
                       reads=["srs32"], writes=["srs32c"])
                sq.add("sync", lambda e: e.dma_start(
                    out=dsumB3[:].rearrange("h (p j) -> (h p) j", p=16),
                    in_=srs32c[:]),
                    reads=["srs32c"], writes=["dsB3"], dma=True)

            def epi_B(d0, hb0):
                o0 = 64 * hb0
                src = AP(dsumB3, 2048 * hb0, [[0, 64], [1, 2048]])
                sq.add("sync", lambda e, src=src, o=o0: e.dma_start(
                    out=srbt[o:o + 64, :], in_=src),
                    reads=["dsB3"], writes=[f"srbt:{hb0}"], dma=True)
                for qc in range(NQC):
                    sq.add("vector", lambda e, qc=qc, o=o0, d=d0: e.tensor_mul(
                        soT[o:o + 64, d, 512 * qc:512 * (qc + 1)],
                        soT[o:o + 64, d, 512 * qc:512 * (qc + 1)],
                        srbt[o:o + 64, 512 * qc:512 * (qc + 1)]),
                        reads=[f"srbt:{hb0}", "soT"], writes=["soT"])

            for h in range(NH):
                esrc = AP(dtau, 384 * h + 127, [[NH * 384 - 1, 128], [1, 240]])
                sq.add("sync", lambda e, h=h, esrc=esrc: e.dma_start(
                    out=sE[:, h, :], in_=esrc),
                    reads=[f"dtau:{h}"], writes=[f"E:{h}"], dma=True)

            def qkproj(d):
                wb = d % 2
                for which, wtag, stile, dst, dsttag, bc, btag in (
                        (0, "wq", swq, sqt, "sqt", sbqc, "bqc"),
                        (1, "wk", swk, skt, "skt", sbkc, "bkc")):
                    for qc in range(NQC):
                        pb = p_s[2 * which + (qc % 2)]
                        pt = f"ps:{2 * which + (qc % 2)}"
                        for c in range(NCH):
                            sq.add("tensor", lambda e, stile=stile, wb=wb, c=c,
                                   qc=qc, pb=pb: e.matmul(
                                       pb[:], stile[:, wb, c, :],
                                       shT[:, c, 512 * qc:512 * (qc + 1)],
                                       start=(c == 0), stop=(c == NCH - 1)),
                                   reads=[f"{wtag}:{wb}", f"hT:{c}:{qc}"],
                                   writes=[pt])
                        sq.add("vector", lambda e, dst=dst, qc=qc, pb=pb, bc=bc,
                               d=d, pt=pt: e.tensor_scalar(
                                   dst[:, 512 * qc:512 * (qc + 1)], pb[:],
                                   bc[:, d:d + 1], None, ALU.add),
                               reads=[pt, btag], writes=[dsttag])

            def vproj_steps(d):
                wb = d % 2
                vb = d % 2
                steps = []

                def memset_ones(vb=vb):
                    sq.add("vector", lambda e, vb=vb: e.memset(
                        sv4[:, vb, :, :, 64:65], 1.0),
                        writes=[f"svones:{vb}"])
                steps.append(memset_ones)
                for t in range(NT):
                    def grp(t=t, wb=wb, vb=vb, d=d):
                        pv = p_v[t % 2]
                        pvt = f"pv:{t % 2}"
                        for c in range(NCH):
                            sq.add("tensor", lambda e, wb=wb, c=c, t=t,
                                   pv=pv: e.matmul(
                                pv[:],
                                shT[:, c, 128 * t:128 * (t + 1)],
                                swv[:, wb, c, :], start=(c == 0), stop=False),
                                reads=[f"wv:{wb}", f"hT:{c}:{t // 4}"],
                                writes=[pvt])
                        sq.add("tensor", lambda e, d=d, pv=pv: e.matmul(
                            pv[:], ones1[:],
                            sbvb[:, 128 * d:128 * (d + 1)],
                            start=False, stop=True),
                            reads=["bvb", "ones1"], writes=[pvt])
                        sq.add("vector", lambda e, t=t, vb=vb, pv=pv: e.tensor_copy(
                            sv4[:, vb, t, :, 0:64],
                            pv[:].rearrange("p (g c) -> p g c", c=64)),
                            reads=[pvt], writes=[f"sv:{vb}:{t}"])
                    steps.append(grp)
                return steps

            prev = None
            for d in range(NP):
                vb = d % 2
                qkproj(d)
                if d == 0:
                    for st in vproj_steps(0):
                        st()
                if d + 2 < NP:
                    wload(d + 2)
                vsteps = vproj_steps(d + 1) if d + 1 < NP else []
                vinj = [0]
                step_idx = [0]
                TOT = sum(4 * qc + 4 for qc in range(NQC))

                for qc in range(NQC):
                    nkt = 4 * qc + 4

                    def emit_qk(hb, kt, qc=qc):
                        qoff = max(0, 128 * kt - 512 * qc)
                        bi = (2 * kt + hb) % 4
                        pb = p_s[bi]
                        o = 64 * hb
                        sq.add("tensor", lambda e, kt=kt, qoff=qoff, pb=pb,
                               o=o, qc=qc: e.matmul(
                                   pb[:, qoff:512],
                                   skt[o:o + 64, 128 * kt:128 * (kt + 1)],
                                   sqt[o:o + 64, 512 * qc + qoff:512 * (qc + 1)],
                                   start=True, stop=True, tile_position=(o, 0)),
                               reads=["sqt", "skt"], writes=[f"ps:{bi}"])

                    def emit_sm(hb, kt, qc=qc, d=d):
                        qoff = max(0, 128 * kt - 512 * qc)
                        bi = (2 * kt + hb) % 4
                        pb = p_s[bi]
                        sl = kt % 2
                        h = 2 * d + hb
                        ptt = f"pt:{sl}:{hb}"
                        sq.add("scalar", lambda e, qoff=qoff, pb=pb, sl=sl,
                               hb=hb: e.activation(
                            sPt[:, sl, hb, qoff:512], pb[:, qoff:512],
                            AF.Exp, bias=EXPB, scale=F1),
                            reads=[f"ps:{bi}"], writes=[ptt])
                        j = kt - 4 * qc
                        if j >= 0:
                            w = min(240, 512 - 128 * j)
                            sq.add("vector", lambda e, j=j, w=w, h=h, sl=sl,
                                   hb=hb: e.tensor_mul(
                                sPt[:, sl, hb, 128 * j:128 * j + w],
                                sPt[:, sl, hb, 128 * j:128 * j + w],
                                sE[:, h, 0:w]),
                                reads=[f"E:{h}", ptt], writes=[ptt])
                        elif j == -1:
                            sq.add("vector", lambda e, h=h, sl=sl,
                                   hb=hb: e.tensor_mul(
                                sPt[:, sl, hb, 0:112], sPt[:, sl, hb, 0:112],
                                sE[:, h, 128:240]),
                                reads=[f"E:{h}", ptt], writes=[ptt])

                    def emit_pv(hb, kt, qc=qc, nkt=nkt, vb=vb):
                        qoff = max(0, 128 * kt - 512 * qc)
                        sl = kt % 2
                        sq.add("tensor", lambda e, kt=kt, qoff=qoff, hb=hb,
                               sl=sl, nkt=nkt, vb=vb: e.matmul(
                                   p_o2[hb][:, qoff:512],
                                   sv[:, vb, kt, 65 * hb:65 * hb + 65],
                                   sPt[:, sl, hb, qoff:512],
                                   start=(kt == 0), stop=(kt == nkt - 1)),
                               reads=[f"pt:{sl}:{hb}", f"sv:{vb}:{kt}",
                                      f"svones:{vb}"],
                               writes=[f"po:{hb}"])

                    emit_qk(0, 0)
                    emit_qk(1, 0)
                    for kt in range(nkt):
                        if kt + 1 < nkt:
                            emit_qk(0, kt + 1)
                            emit_qk(1, kt + 1)
                        for hb in (0, 1):
                            emit_sm(hb, kt)
                            emit_pv(hb, kt)
                        step_idx[0] += 1
                        if vsteps:
                            want = (step_idx[0] * len(vsteps)) // TOT
                            while vinj[0] < want:
                                vsteps[vinj[0]]()
                                vinj[0] += 1

                    if prev is not None:
                        if qc == 0:
                            epi_A()
                        elif qc == 1:
                            epi_B(prev, 0)
                        elif qc == 2:
                            epi_B(prev, 1)

                    for hb in (0, 1):
                        sq.add("vector", lambda e, qc=qc, hb=hb, d=d: e.tensor_copy(
                            soT[64 * hb:64 * hb + 64, d,
                                512 * qc:512 * (qc + 1)],
                            p_o2[hb][0:64, :]),
                            reads=[f"po:{hb}"], writes=["soT"])
                        sq.add("scalar", lambda e, qc=qc, hb=hb: e.activation(
                            srowsb[hb][0:1, qc, :], p_o2[hb][64:65, :],
                            AF.Identity, bias=F0, scale=F1),
                            reads=[f"po:{hb}"], writes=[f"srb:{hb}"])

                while vsteps and vinj[0] < len(vsteps):
                    vsteps[vinj[0]]()
                    vinj[0] += 1
                for hb in (0, 1):
                    sq.add("sync", lambda e, hb=hb: e.dma_start(
                        out=dsumB2[hb:hb + 1, :],
                        in_=srowsb[hb][:].rearrange("o q j -> o (q j)")),
                        reads=[f"srb:{hb}"], writes=["dsB2"], dma=True)
                prev = d

            epi_A()
            epi_B(prev, 0)
            epi_B(prev, 1)

            for t in range(NT):
                for half in range(2):
                    pb = p_s[(2 * t + half) % 2]
                    pt = f"ps:{(2 * t + half) % 2}"
                    for c in range(NCH):
                        sq.add("tensor", lambda e, t=t, half=half, c=c, pb=pb: e.matmul(
                            pb[:], soT[:, c, 128 * t:128 * (t + 1)],
                            swo[:, c, 512 * half:512 * (half + 1)],
                            start=(c == 0), stop=False),
                            reads=["soT", "swo"], writes=[pt])
                    sq.add("tensor", lambda e, half=half, pb=pb: e.matmul(
                        pb[:], ones1[:], sbob[:, 512 * half:512 * (half + 1)],
                        start=False, stop=True),
                        reads=["bob", "ones1"], writes=[pt])
                    sq.add("vector", lambda e, half=half, pb=pb, pt=pt: e.scalar_tensor_tensor(
                        p_s[2][:], pb[:], 1.0, gat[:, 512 * half:512 * (half + 1)],
                        ALU.mult, ALU.mult),
                        reads=[pt, "gat"], writes=["ps:2"])
                    sq.add("vector", lambda e, t=t, half=half: e.tensor_add(
                        sx[:, t, 512 * half:512 * (half + 1)],
                        sx[:, t, 512 * half:512 * (half + 1)], p_s[2][:]),
                        reads=["ps:2"], writes=[f"x:{t}"])
                if half == 1:
                    ln_stats(t)

            ln_batch()

        sq.barrier()

        with ExitStack() as a3:
            sq.enabled = (upto >= 3)

            def sb3(name, shape, dtp):
                return a3.enter_context(nc.sbuf_tensor(name, shape, dtp, side="right"))

            def ps3(name, shape):
                return a3.enter_context(nc.psum_tensor(name, shape, F32))

            spk = sb3("spk", [128, NCH, 256], I32)
            swf = sb3("swf", [128, NCH, D], BF16)
            sy3 = sb3("sy3", [128, 2, D], BF16)
            sh2 = sb3("sh2", [128, 2, D], BF16)
            st32 = sb3("st32", [128, 2, D], F32)
            sqi = sb3("sqi", [128, 2, D], I32)
            sei = sb3("sei", [128, 256], I32)
            gbw = sb3("gbw", [128, D], BF16)
            gbiast = sb3("gbiast", [128, D], BF16)
            a2t = sb3("a2t", [128, D], BF16)
            b2t = sb3("b2t", [128, D], BF16)
            srowb3 = sb3("srowb3", [1, D], BF16)
            p_f = [ps3(f"p_f{i}", [128, 512]) for i in range(2)]
            p_g3 = ps3("p_g3", [128, 512])
            p_tr3 = [a3.enter_context(nc.psum_tensor(f"p_tr3{i}", [128, 128], BF16))
                     for i in range(2)]

            def rebuild(dst, tag, slot, plus1=False):
                sq.add("sync", lambda e, slot=slot: e.dma_start(
                    out=srowb3[:], in_=drow2[:, D * slot:D * (slot + 1)]),
                    reads=[f"dr2:{slot}"], writes=["srowb3"], dma=True)
                for half in range(2):
                    sq.add("tensor", lambda e, half=half: e.matmul(
                        p_g3[:], ones1[:], srowb3[:, 512 * half:512 * (half + 1)],
                        start=True, stop=True),
                        reads=["srowb3", "ones1"], writes=["pg3"])
                    bias = F1 if plus1 else F0
                    sq.add("scalar", lambda e, half=half, dst=dst, bias=bias: e.activation(
                        dst[:, 512 * half:512 * (half + 1)], p_g3[:], AF.Identity,
                        bias=bias, scale=F1),
                        reads=["pg3"], writes=[tag])

            rebuild(gbw, "gbw", 0)
            rebuild(gbiast, "gbiast", 1)
            rebuild(a2t, "a2t", 2, plus1=True)
            rebuild(b2t, "b2t", 3)

            for c in range(NCH):
                sq.add("sync", lambda e, c=c: e.dma_start(
                    out=spk[:, c, :], in_=pkt[128 * c:128 * (c + 1), :]),
                    writes=["spk"], dma=True)
            def unpack_chunk(c):
                for s in range(4):
                    sq.add("vector", lambda e, c=c, s=s: e.tensor_scalar(
                        sei[:], spk[:, c, :], 2 * s, 3,
                        ALU.logical_shift_right, ALU.bitwise_and),
                        reads=["spk"], writes=["sei"])
                    sq.add("vector", lambda e, c=c, s=s: e.tensor_scalar(
                        swf[:, c, 256 * s:256 * (s + 1)], sei[:], 1, None,
                        ALU.subtract),
                        reads=["sei"], writes=[f"swf:{c}"])
                sq.add("gpsimd", lambda e, c=c: e.tensor_mul(
                    swf[:, c, :], swf[:, c, :], gbw[:]),
                    reads=["gbw", f"swf:{c}"], writes=[f"swf:{c}"])

            for c in range(NCH):
                unpack_chunk(c)
            def q_stageA(t):
                b = t % 2
                ln_apply(t, sy3[:, b, :], f"sy:{b}")
                sq.add("vector", lambda e, b=b: e.tensor_mul(
                    sh2[:, b, :], sy3[:, b, :], a2t[:]),
                    reads=[f"sy:{b}", "a2t"], writes=[f"sh2:{b}"])
                sq.add("vector", lambda e, b=b: e.tensor_add(
                    sh2[:, b, :], sh2[:, b, :], b2t[:]),
                    reads=["b2t", f"sh2:{b}"], writes=[f"sh2:{b}"])
                sq.add("vector", lambda e, b=b: e.reduce_max(
                    stmp1[:, b:b + 1], sh2[:, b, :], AXX,
                    apply_absolute_value=True),
                    reads=[f"sh2:{b}"], writes=[f"t1:{b}"])
                sq.add("vector", lambda e, b=b: e.tensor_scalar_max(
                    stmp1[:, b:b + 1], stmp1[:, b:b + 1], 1e-5),
                    reads=[f"t1:{b}"], writes=[f"t1:{b}"])
                sq.add("vector", lambda e, t=t, b=b: e.tensor_scalar(
                    skap[:, t:t + 1], stmp1[:, b:b + 1], s_swb[:], 1.0 / 127.0,
                    ALU.mult, ALU.mult),
                    reads=[f"t1:{b}", "swb"], writes=[f"kap:{t}"])
                sq.add("vector", lambda e, b=b: e.reciprocal(
                    stmp1b[:, b:b + 1], stmp1[:, b:b + 1]),
                    reads=[f"t1:{b}"], writes=[f"t1b:{b}"])
                sq.add("vector", lambda e, t=t, b=b: e.tensor_scalar_mul(
                    ssc[:, t:t + 1], stmp1b[:, b:b + 1], 127.0),
                    reads=[f"t1b:{b}"], writes=[f"qsc:{t}"])

            def q_stageB(t):
                b = t % 2
                sq.add("scalar", lambda e, t=t, b=b: e.activation(
                    st32[:, b, :], sh2[:, b, :], AF.Identity,
                    scale=ssc[:, t:t + 1], bias=F0),
                    reads=[f"sh2:{b}", f"qsc:{t}"], writes=[f"st32:{b}"])
                sq.add("vector", lambda e, b=b: e.tensor_copy(
                    sqi[:, b, :], st32[:, b, :]),
                    reads=[f"st32:{b}"], writes=[f"sqi:{b}"])
                sq.add("vector", lambda e, b=b: e.tensor_copy(
                    sy3[:, b, :], sqi[:, b, :]),
                    reads=[f"sqi:{b}"], writes=[f"sy:{b}"])
                for c in range(NCH):
                    pp = c % 2
                    sq.add("tensor", lambda e, b=b, c=c, pp=pp: e.transpose(
                        p_tr3[pp][:], sy3[:, b, 128 * c:128 * (c + 1)], sid128[:]),
                        reads=[f"sy:{b}", "sid"], writes=[f"ptr:{pp}"])
                    if c % 2 == 0:
                        sq.add("scalar", lambda e, t=t, c=c, pp=pp: e.activation(
                            shT[:, c, 128 * t:128 * (t + 1)], p_tr3[pp][:],
                            AF.Identity, bias=F0, scale=F1),
                            reads=[f"ptr:{pp}"], writes=[f"hT:{c}:{t // 4}"])
                    else:
                        sq.add("vector", lambda e, t=t, c=c, pp=pp: e.tensor_copy(
                            shT[:, c, 128 * t:128 * (t + 1)], p_tr3[pp][:]),
                            reads=[f"ptr:{pp}"], writes=[f"hT:{c}:{t // 4}"])

            q_stageA(0)
            for t in range(NT):
                if t + 1 < NT:
                    q_stageA(t + 1)
                q_stageB(t)

            for t in range(NT):
                for half in range(2):
                    pb = p_f[(2 * t + half) % 2]
                    pt = f"pf:{(2 * t + half) % 2}"
                    for c in range(NCH):
                        sq.add("tensor", lambda e, t=t, half=half, c=c, pb=pb: e.matmul(
                            pb[:], shT[:, c, 128 * t:128 * (t + 1)],
                            swf[:, c, 512 * half:512 * (half + 1)],
                            start=(c == 0), stop=(c == NCH - 1)),
                            reads=[f"hT:{c}:{t // 4}", f"swf:{c}"], writes=[pt]),
                    sq.add("vector", lambda e, t=t, half=half, pb=pb, pt=pt: e.scalar_tensor_tensor(
                        p_g3[:], pb[:], skap[:, t:t + 1],
                        gbiast[:, 512 * half:512 * (half + 1)],
                        ALU.mult, ALU.add),
                        reads=[pt, f"kap:{t}", "gbiast"], writes=["pg3"])
                    sq.add("vector", lambda e, t=t, half=half: e.tensor_add(
                        sx[:, t, 512 * half:512 * (half + 1)],
                        sx[:, t, 512 * half:512 * (half + 1)], p_g3[:]),
                        reads=["pg3"], writes=[f"x:{t}"])

                sq.add("sync", lambda e, t=t: e.dma_start(
                    out=out[128 * t:128 * (t + 1), :], in_=sx[:, t, :]),
                    reads=[f"x:{t}"], writes=[f"out:{t}"], dma=True)
            sq.add("vector", lambda e: e.engine_nop(),
                   reads=[f"out:{t}" for t in range(NT)], writes=["done"])

        block = ctx.enter_context(nc.Block())
        sq.build(block, sems, serial=serial)
    return nc


_CACHED = {}


def _get_nc():
    if "nc" not in _CACHED:
        _CACHED["nc"] = build_nc()
    return _CACHED["nc"]


def kernel(L0_attn_x, L0_attn_Wq, L0_attn_bq, L0_attn_Wk, L0_attn_bk,
           L0_attn_Wv, L0_attn_bv, L0_attn_Wo, L0_attn_bo, L0_attn_rel_emb,
           L0_ffn_packed_weight, L0_ffn_weight_scale, L0_ffn_bias, L0_c,
           L0_wmod):
    x = np.ascontiguousarray(np.asarray(L0_attn_x, np.float32))
    wqT = np.ascontiguousarray(np.asarray(L0_attn_Wq, np.float32).T)
    wkT = np.ascontiguousarray(np.asarray(L0_attn_Wk, np.float32).T)
    wvT = np.ascontiguousarray(np.asarray(L0_attn_Wv, np.float32).T)
    woT = np.ascontiguousarray(np.asarray(L0_attn_Wo, np.float32).T)
    wmv = np.ascontiguousarray(np.asarray(L0_wmod, np.float32))
    pkT = np.ascontiguousarray(
        np.asarray(L0_ffn_packed_weight, np.int32).reshape(256, 1024).T)
    relv = np.ascontiguousarray(np.asarray(L0_attn_rel_emb, np.float32))
    ohv = _oh_const()
    c = np.asarray(L0_c, np.float32)
    bqcv = np.ascontiguousarray(np.asarray(L0_attn_bq, np.float32).reshape(8, 128).T)
    bkcv = np.ascontiguousarray(np.asarray(L0_attn_bk, np.float32).reshape(8, 128).T)
    bvr = np.ascontiguousarray(np.asarray(L0_attn_bv, np.float32).reshape(1, D))
    borv = np.ascontiguousarray(np.asarray(L0_attn_bo, np.float32).reshape(1, D))
    fbrv = np.ascontiguousarray(np.asarray(L0_ffn_bias, np.float32).reshape(1, D))
    wscv = np.asarray(L0_ffn_weight_scale, np.float32).reshape(1, 1)
    eye16 = np.eye(16, dtype=np.float32)
    eye128 = np.eye(128, dtype=ml_dtypes.bfloat16)

    nc = _get_nc()
    in_maps = []
    for b in range(8):
        in_maps.append({
            "x": x[b], "wq": wqT, "wk": wkT, "wv": wvT, "wo": woT,
            "wm": wmv, "pk": pkT,
            "ccol": np.ascontiguousarray(c[b].reshape(8, 128).T),
            "bqc": bqcv, "bkc": bkcv, "bvr": bvr, "bor": borv, "fbr": fbrv,
            "rel": relv, "oh": ohv, "wsc": wscv, "idf": eye16,
            "eye128": eye128,
        })
    res = run_bass_kernel_spmd(nc, in_maps, core_ids=list(range(8)))
    return np.stack([r["out"] for r in res.results], axis=0)
